# version 08
# baseline (speedup 1.0000x reference)
import numpy as np
import concourse.bass as bass
import concourse.mybir as mybir
from concourse.bass_utils import run_bass_kernel_spmd
from concourse.aot_env import allow_ici_all_to_all
from contextlib import ExitStack

F32 = mybir.dt.float32
BF16 = mybir.dt.bfloat16
AF = mybir.ActivationFunctionType
ALU = mybir.AluOpType
AX = mybir.AxisListType

S, E, H, D = 2048, 1024, 16, 64
HP, ET, ST = H // 2, E // 128, S // 128
PC, NT = S // 8, 8
EPS = 1e-5
SM_SCALE = float(1.0 / np.sqrt(D))

ATTN_W = ["wqT", "wkT", "wvT", "woT"]
FFN_W = ["fwcT", "fucT", "fwiT", "fuiT", "fwfT", "fufT", "fwoT", "fuoT"]
BIAS_NAMES = ["bq", "bk", "bv", "bo", "fbf", "fbi", "fbc", "fbo", "finc"]
BP = {"bq": 0, "bk": 1, "fbf": 2, "fbi": 3, "fbc": 4, "finc": 5}


def build(dbg=False):
    nc = bass.Bass("TRN2", target_bir_lowering=False, debug=False)

    x_d = nc.dram_tensor("x", [S, E], F32, kind="ExternalInput").ap()
    wT_d = {nm: nc.dram_tensor(nm, [E, E], F32, kind="ExternalInput").ap()
            for nm in ATTN_W + FFN_W}
    bias_d = {nm: nc.dram_tensor("b_" + nm, [E], F32, kind="ExternalInput").ap()
              for nm in BIAS_NAMES}
    out_d = nc.dram_tensor("out", [NT, PC, E], F32, kind="ExternalOutput").ap()
    x1a_d = nc.dram_tensor("x1a", [S // 2, E], F32).ap()
    x1b_d = nc.dram_tensor("x1bb", [S // 2, E], F32).ap()
    x1ga_d = nc.dram_tensor("x1ga", [S // 2, E], F32).ap()
    x1gb_d = nc.dram_tensor("x1gb", [S // 2, E], F32).ap()
    sums_d = nc.dram_tensor("sumsd", [128, 256], F32).ap()
    recip_d = nc.dram_tensor("recipd", [128, 256], BF16).ap()

    es = ExitStack()
    sb = lambda name, shape, dt: es.enter_context(nc.sbuf_tensor(name, shape, dt))
    sem = lambda name: es.enter_context(nc.semaphore(name))

    s_x = sb("s_x", [128, 2048], F32)
    s_scr = sb("s_scr", [128, 2048], F32)
    A1 = sb("A1", [128, 16384], BF16)
    A2 = sb("A2", [128, 16384], BF16)
    A3 = sb("A3", [128, 16384], BF16)
    A4 = sb("A4", [128, 16640], BF16)
    s_W = sb("s_W", [128, 16384], BF16)
    s_wst = sb("s_wst", [128, 2048], F32)
    s_P = sb("s_P", [128, 4096], BF16)
    s_sq = sb("s_sq", [128, 2048], BF16)
    s_cb = sb("s_cb", [128, 2048], BF16)
    s_sum2r = sb("s_sum2r", [128, 256], BF16)
    s_bp = sb("s_bp", [128, 48], F32)
    s_bB = sb("s_bB", [128, 16], F32)
    s_stat = sb("s_stat", [128, 32], F32)
    s_id = sb("s_id", [128, 128], F32)
    s_eAB = sb("s_eAB", [128, 2], BF16)
    s_E2T = sb("s_E2T", [128, 128], BF16)
    s_on1 = sb("s_on1", [1, 128], F32)
    s_nrm = sb("s_nrm", [1, 96], F32)
    s_eps = sb("s_eps", [128, 1], F32)
    s_sumsc = s_scr[0:1, 0:2048]
    s_sum2 = s_scr[:, 0:256]
    s_bb = s_P[:, 0:1024]
    s_tmpf = s_wst[:, 0:256]
    s_row = s_scr[0:1, 0:1024]
    g_tmp = s_sum2r[:, 0:256]

    r3 = lambda t, w: t.rearrange("p (a b) -> p a b", b=w)
    xlnT = r3(A1[:, :], 2048)
    oTu = r3(A1[:, :], 2048)
    wfTb, ufTb = r3(A1[:, 0:8192], 1024), r3(A1[:, 8192:16384], 1024)
    qT = r3(A2[:, :], 2048)
    oTn = r3(A2[:, :], 2048)
    xlnFT = r3(A2[:, :], 2048)
    kT = r3(A3[:, :], 2048)
    wiTb, uiTb = r3(A3[:, 0:8192], 1024), r3(A3[:, 8192:16384], 1024)
    v_aug = r3(A4[:, :], 1040)
    wcTb, ucTb = r3(A4[:, 0:8192], 1024), r3(A4[:, 8192:16384], 1024)
    sWt = [r3(s_W[:, 0:8192], 1024), r3(s_W[:, 8192:16384], 1024)]
    uoTb, woTb = sWt[0], sWt[1]
    cT = r3(s_x[:, :], 256)
    cTb = r3(s_cb[:, :], 256)
    gate = {"f": r3(s_P[:, 0:2048], 256), "i": r3(s_P[:, 2048:4096], 256),
            "c": r3(s_sq[:, 0:2048], 256)}
    U_plus = r3(s_wst[:, :], 1024)

    psA = es.enter_context(nc.psum_tensor("psA", [128, 4096], F32))
    bank = lambda b: psA[:, 512 * b:512 * (b + 1)]
    bank2 = lambda b: psA[:, 512 * b:512 * (b + 2)]

    dG, dSY, ccS, dS2 = sem("dG"), sem("dSY"), sem("ccS"), sem("dS2")
    ccS2 = sem("ccS2")
    SEMS = ["gI", "vCh", "sCh", "gCh", "rq", "rs",
            "p1a", "p1tr", "p1e", "vW", "p2t", "p2v",
            "stS", "stT", "stV", "tM", "sM", "vM", "p5t", "p5v",
            "p6t", "p6a", "p6e", "p8a", "p8tr", "p8e",
            "p9t", "p9s", "p9v", "p10t", "p10a", "p10h", "p10e"]
    sm = {n: sem(n) for n in SEMS}
    C = {n: 0 for n in SEMS}
    C["dG"] = 0
    C["dSY"] = 0
    C["dS2"] = 0

    ops = {k: [] for k in ["tensor", "scalar", "vector", "gpsimd", "sync"]}
    ENG_OF = {"tensor": "tensor", "scalar": "scalar", "vector": "vector",
              "gpsimd": "gpsimd", "sync": "sync"}

    def on(eng):
        def deco(f):
            ops[eng].append(f)
        return deco

    def wait(eng, s, v):
        h = sm[s] if isinstance(s, str) else s

        @on(eng)
        def _(e, h=h, v=v):
            e.wait_ge(h, v)

    def inc(s, n=1):
        C[s] += n
        return sm[s], C[s]

    def gdma(out, in_):
        prev = C["dG"]
        C["dG"] += 16
        t = C["dG"]

        @on("gpsimd")
        def _(e, out=out, in_=in_, prev=prev):
            if prev:
                e.wait_ge(dG, prev)
            e.dma_start(out=out, in_=in_).then_inc(dG, 16)
        return t

    def sydma(out, in_):
        prev = C["dSY"]
        C["dSY"] += 16
        t = C["dSY"]

        @on("sync")
        def _(e, out=out, in_=in_, prev=prev):
            if prev:
                e.wait_ge(dSY, prev)
            e.dma_start(out=out, in_=in_).then_inc(dSY, 16)
        return t

    def scdma(out, in_):
        prev = C["dS2"]
        C["dS2"] += 16
        t = C["dS2"]

        @on("scalar")
        def _(e, out=out, in_=in_, prev=prev):
            if prev:
                e.wait_ge(dS2, prev)
            e.dma_start(out=out, in_=in_).then_inc(dS2, 16)
        return t

    h_gCh, v_gCh1 = inc("gCh")

    @on("gpsimd")
    def _(e, tgt=v_gCh1):
        e.memset(s_id[:, :], 0.0)
        e.memset(s_on1[:, :], 1.0)
        e.memset(s_eps[:, :], EPS)
        e.memset(v_aug.rearrange("p a (h c) -> p a h c", c=65)[:, :, :, 64:65],
                 1.0).then_inc(sm["gCh"], 1)
        e.wait_ge(sm["gCh"], tgt)
        e.affine_select(out=s_id[:, :], in_=s_id[:, :], compare_op=ALU.not_equal,
                        fill=1.0, base=0, pattern=[[-1, 128]], channel_multiplier=1)
        e.iota(s_stat[:, 0:1], pattern=[[0, 1]], base=0, channel_multiplier=1,
               allow_small_or_imprecise_dtypes=True)
        e.iota(s_scr[0:2, 0:128], pattern=[[1, 128]], base=0,
               channel_multiplier=-64,
               allow_small_or_imprecise_dtypes=True).then_inc(sm["gI"], 1)
    C["gI"] += 1
    wait("vector", "gI", 1)
    _, v_e2a = inc("vCh")
    _, v_e2b = inc("vCh")

    @on("vector")
    def _(e, v_e2a=v_e2a, v_e2b=v_e2b):
        e.tensor_scalar(s_eAB[:, 0:1], s_stat[:, 0:1], 64.0, None, op0=ALU.is_lt)
        e.tensor_scalar(s_eAB[:, 1:2], s_stat[:, 0:1], 64.0, None, op0=ALU.is_ge)
        e.tensor_scalar(s_scr[0:2, 128:256], s_scr[0:2, 0:128], 0.0, None,
                        op0=ALU.is_ge)
        e.tensor_scalar(s_scr[0:2, 256:384], s_scr[0:2, 0:128], 64.0, None,
                        op0=ALU.is_lt).then_inc(sm["vCh"], 1)
        e.wait_ge(sm["vCh"], v_e2a)
        e.tensor_tensor(s_E2T[0:2, :], s_scr[0:2, 128:256], s_scr[0:2, 256:384],
                        op=ALU.mult).then_inc(sm["vCh"], 1)
        e.wait_ge(sm["vCh"], v_e2b)
        e.tensor_copy(s_E2T[64:66, :], s_E2T[0:2, :]).then_inc(sm["vCh"], 1)
    inc("vCh")

    wcast_done = {}
    W_CHUNK = [0]

    def load_weight(name, slot_or_dst, queue=None, dep=None):
        q = queue or sydma
        qeng = {"sydma": "sync", "gdma": "gpsimd", "scdma": "scalar"}[
            "sydma" if q is sydma else ("gdma" if q is gdma else "scdma")]
        dsem = dSY if q is sydma else (dG if q is gdma else dS2)
        dst = sWt[slot_or_dst] if isinstance(slot_or_dst, int) else slot_or_dst
        for et in range(ET):
            g = W_CHUNK[0]
            if g >= 2:
                wait(qeng, "vW", g - 1)
            if dep is not None and et == 0:
                wait(qeng, dep[0], dep[1])
            t = q(s_wst[:, (g % 2) * 1024:(g % 2 + 1) * 1024],
                  wT_d[name][et * 128:(et + 1) * 128, :])
            wait("vector", dsem, t)
            inc("vW")

            @on("vector")
            def _(e, et=et, g=g, dst=dst):
                e.tensor_copy(dst[:, et, :],
                              s_wst[:, (g % 2) * 1024:(g % 2 + 1) * 1024]
                              ).then_inc(sm["vW"], 1)
            W_CHUNK[0] += 1
        wcast_done[name] = C["vW"]

    load_weight("wqT", 0)

    def ln_chain(src_ap, dst_ap, apply_sem):
        _, t1 = inc("vCh")

        @on("vector")
        def _(e, src=src_ap, t1=t1):
            e.bn_stats(s_stat[:, 0:6], src[:, 0:512])
            e.bn_stats(s_stat[:, 6:12], src[:, 512:1024]).then_inc(sm["vCh"], 1)
            e.wait_ge(sm["vCh"], t1)
            e.bn_aggr(s_stat[:, 12:14], s_stat[:, 0:12]).then_inc(sm["rq"], 1)
        _, rq_t = inc("rq")
        wait("scalar", "rq", rq_t)
        _, s1 = inc("sCh")

        @on("scalar")
        def _(e, s1=s1):
            e.activation(s_stat[:, 15:16], s_stat[:, 13:14], AF.Ln,
                         bias=s_eps[:, :]).then_inc(sm["sCh"], 1)
            e.wait_ge(sm["sCh"], s1)
            e.activation(s_stat[:, 14:15], s_stat[:, 15:16], AF.Exp,
                         scale=-0.5).then_inc(sm["rs"], 1)
        _, rs_t = inc("rs")
        wait("vector", "rs", rs_t)
        _, t2 = inc("vCh")
        _, ap_t = inc(apply_sem)

        @on("vector")
        def _(e, src=src_ap, dst=dst_ap, t2=t2, apply_sem=apply_sem):
            e.tensor_scalar(s_stat[:, 15:16], s_stat[:, 12:13], s_stat[:, 14:15],
                            -1.0, op0=ALU.mult, op1=ALU.mult).then_inc(sm["vCh"], 1)
            e.wait_ge(sm["vCh"], t2)
            e.tensor_scalar(dst, src, s_stat[:, 14:15], s_stat[:, 15:16],
                            op0=ALU.mult, op1=ALU.add).then_inc(sm[apply_sem], 1)
        return ap_t

    def ln_transpose_pipe(dstT, dma_tgts, dma_sem, ap_sem, tr_sem, ev_sem,
                          base_tr, base_ev, cols=None):
        if cols is None:
            cols = [128 * t for t in range(ST)]
        """Depth-2 software pipeline over ST tiles:
        V-step st: apply(st-1) | stats+aggr(st) | psum-evac(st-1).
        Tiles live in s_x slots (st%2); f32 LN output in s_scr slots (st%2);
        8 f32 transposes per tile into psum banks 0-1; bf16 evac into dstT.
        """
        rq_t = {st: C["rq"] + st + 1 for st in range(ST)}
        rs_t = {st: C["rs"] + st + 1 for st in range(ST)}
        ln_t = {st: C["sCh"] + st + 1 for st in range(ST)}
        C["rq"] += ST
        C["rs"] += ST
        C["sCh"] += ST
        for st in range(ST + 1):
            @on("scalar")
            def _(e, st=st):
                if st >= 1:
                    p = 16 * ((st - 1) % 2)
                    e.wait_ge(sm["sCh"], ln_t[st - 1])
                    e.activation(s_stat[:, p + 14:p + 15], s_stat[:, p + 15:p + 16],
                                 AF.Exp, scale=-0.5).then_inc(sm["rs"], 1)
                if st < ST:
                    p = 16 * (st % 2)
                    e.wait_ge(sm["rq"], rq_t[st])
                    e.activation(s_stat[:, p + 15:p + 16], s_stat[:, p + 13:p + 14],
                                 AF.Ln, bias=s_eps[:, :]).then_inc(sm["sCh"], 1)
        for st in range(ST + 1):
            if st >= 1:
                stp = st - 1
                p = 16 * (stp % 2)
                wait("vector", "rs", rs_t[stp])
                if stp >= 2:
                    wait("vector", tr_sem, base_tr + 2 * (stp - 2) + 2)
                _, m1 = inc("vCh")
                _, ap_v = inc(ap_sem)

                @on("vector")
                def _(e, stp=stp, p=p, m1=m1, ap_sem=ap_sem):
                    e.tensor_scalar(s_stat[:, p + 15:p + 16], s_stat[:, p + 12:p + 13],
                                    s_stat[:, p + 14:p + 15], -1.0,
                                    op0=ALU.mult, op1=ALU.mult).then_inc(sm["vCh"], 1)
                    e.wait_ge(sm["vCh"], m1)
                    e.tensor_scalar(s_scr[:, (stp % 2) * 1024:(stp % 2 + 1) * 1024],
                                    s_x[:, (stp % 2) * 1024:(stp % 2 + 1) * 1024],
                                    s_stat[:, p + 14:p + 15],
                                    s_stat[:, p + 15:p + 16],
                                    op0=ALU.mult, op1=ALU.add
                                    ).then_inc(sm[ap_sem], 1)
                wait("tensor", ap_sem, ap_v)
                for g in range(2):
                    if stp >= 1:
                        wait("tensor", ev_sem, base_ev + 2 * (stp - 1) + g + 1)
                    _, tr_v = inc(tr_sem)

                    @on("tensor")
                    def _(e, stp=stp, g=g, tr_sem=tr_sem):
                        xln = s_scr[:, (stp % 2) * 1024:(stp % 2 + 1) * 1024]
                        for et in range(4 * g, 4 * g + 4):
                            ins = e.transpose(psA[:, g * 512 + (et % 4) * 128:
                                                  g * 512 + (et % 4) * 128 + 128],
                                              xln[:, et * 128:(et + 1) * 128],
                                              s_id[:, :])
                        ins.then_inc(sm[tr_sem], 1)
            if st < ST:
                wait("vector", dma_sem, dma_tgts[st])
                _, sb_v = inc("vCh")

                @on("vector")
                def _(e, st=st, sb_v=sb_v):
                    p = 16 * (st % 2)
                    xt = s_x[:, (st % 2) * 1024:(st % 2 + 1) * 1024]
                    e.bn_stats(s_stat[:, p:p + 6], xt[:, 0:512])
                    e.bn_stats(s_stat[:, p + 6:p + 12],
                               xt[:, 512:1024]).then_inc(sm["vCh"], 1)
                    e.wait_ge(sm["vCh"], sb_v)
                    e.bn_aggr(s_stat[:, p + 12:p + 14],
                              s_stat[:, p:p + 12]).then_inc(sm["rq"], 1)
            if st >= 1:
                stp = st - 1
                for g in range(2):
                    wait("vector", tr_sem, base_tr + 2 * stp + g + 1)
                    _, ev_v = inc(ev_sem)

                    @on("vector")
                    def _(e, stp=stp, g=g, dstT=dstT, ev_sem=ev_sem,
                          c0=cols[st - 1]):
                        for et in range(4 * g, 4 * g + 4):
                            ins = e.tensor_copy(
                                dstT[:, et, c0:c0 + 128],
                                psA[:, g * 512 + (et % 4) * 128:
                                    g * 512 + (et % 4) * 128 + 128])
                        ins.then_inc(sm[ev_sem], 1)

    xdma = []
    for st in range(ST):
        if st >= 2:
            wait("gpsimd", "p1a", st - 1)
        xdma.append(gdma(s_x[:, (st % 2) * 1024:(st % 2 + 1) * 1024],
                         x_d[st * 128:(st + 1) * 128, :]))
    ln_transpose_pipe(xlnT, xdma, dG, "p1a", "p1tr", "p1e", 0, 0)

    load_weight("wkT", 1)
    for nm in ["bq", "bk"]:
        for j in range(ET):
            sydma(s_bp[:, BP[nm] * 8 + j:BP[nm] * 8 + j + 1],
                  bias_d[nm][j * 128:(j + 1) * 128].rearrange("(p o) -> p o", o=1))
    BQBK_T = C["dSY"]
    for nm in ["fbf", "fbi", "fbc", "finc"]:
        for j in range(ET):
            sydma(s_bp[:, BP[nm] * 8 + j:BP[nm] * 8 + j + 1],
                  bias_d[nm][j * 128:(j + 1) * 128].rearrange("(p o) -> p o", o=1))
    FB_T = C["dSY"]

    def bias_bcast(row_t):
        wait("tensor", dSY, row_t)
        _, t_t = inc("p2t")

        @on("tensor")
        def _(e):
            e.matmul(bank(6), s_on1[:, :], s_row[0:1, 0:512], start=True, stop=True)
            e.matmul(bank(7), s_on1[:, :], s_row[0:1, 512:1024],
                     start=True, stop=True).then_inc(sm["p2t"], 1)
        wait("vector", "p2t", t_t)
        _, v_t = inc("p2v")

        @on("vector")
        def _(e):
            e.tensor_copy(s_bb[:, 0:512], bank(6))
            e.tensor_copy(s_bb[:, 512:1024], bank(7)).then_inc(sm["p2v"], 1)
        return v_t

    wait("vector", dSY, BQBK_T)

    EVAC_V = []

    def proj_fmajor(dst, wname, bname, scale2):
        slot = {"wqT": 0, "wkT": 1, "wvT": 0, "woT": 1}[wname]
        wait("tensor", "vW", wcast_done[wname])
        for ch in range(4):
            wait("tensor", "p1e", 8 * (ch + 1))
            for f in range(ET):
                pj = len(EVAC_V)
                pbank = 2 + (pj % 4)
                if pj >= 4:
                    wait("tensor", "p2v", EVAC_V[pj - 4])
                _, t_t = inc("p2t")

                @on("tensor")
                def _(e, ch=ch, f=f, slot=slot, pbank=pbank):
                    for et in range(ET):
                        ins = e.matmul(bank(pbank),
                                       sWt[slot][:, et, f * 128:(f + 1) * 128],
                                       xlnT[:, et, ch * 512:(ch + 1) * 512],
                                       start=(et == 0), stop=(et == ET - 1))
                    ins.then_inc(sm["p2t"], 1)
                wait("vector", "p2t", t_t)
                bcol = BP[bname] * 8 + f
                _, v_t = inc("p2v")
                EVAC_V.append(v_t)

                @on("vector")
                def _(e, ch=ch, f=f, dst=dst, bcol=bcol, scale2=scale2,
                      pbank=pbank):
                    if scale2 is None:
                        e.tensor_scalar(dst[:, f, ch * 512:(ch + 1) * 512],
                                        bank(pbank), s_bp[:, bcol:bcol + 1], None,
                                        op0=ALU.add).then_inc(sm["p2v"], 1)
                    else:
                        e.tensor_scalar(dst[:, f, ch * 512:(ch + 1) * 512],
                                        bank(pbank), s_bp[:, bcol:bcol + 1], scale2,
                                        op0=ALU.add, op1=ALU.mult
                                        ).then_inc(sm["p2v"], 1)

    proj_fmajor(qT, "wqT", "bq", SM_SCALE)
    load_weight("wvT", 0, dep=("p2t", C["p2t"]))
    proj_fmajor(kT, "wkT", "bk", None)
    load_weight("woT", 1, dep=("p2t", C["p2t"]))

    wait("sync", "p1e", 32)
    BV_ROW_T = sydma(s_row, bias_d["bv"][:].rearrange("(o n) -> o n", o=1))
    BV_BC_V = bias_bcast(BV_ROW_T)
    wait("vector", "p2v", BV_BC_V)

    wait("tensor", "vW", wcast_done["wvT"])
    for st in range(ST):
        for ch in range(2):
            pj = len(EVAC_V)
            pbank = 2 + (pj % 4)
            if pj >= 4:
                wait("tensor", "p2v", EVAC_V[pj - 4])
            _, t_t = inc("p2t")

            @on("tensor")
            def _(e, st=st, ch=ch, pbank=pbank):
                for et in range(ET):
                    ins = e.matmul(bank(pbank), xlnT[:, et, st * 128:(st + 1) * 128],
                                   sWt[0][:, et, ch * 512:(ch + 1) * 512],
                                   start=(et == 0), stop=(et == ET - 1))
                ins.then_inc(sm["p2t"], 1)
            wait("vector", "p2t", t_t)
            _, v_t = inc("p2v")
            EVAC_V.append(v_t)

            @on("vector")
            def _(e, st=st, ch=ch, pbank=pbank):
                dst = v_aug.rearrange("p a (h c) -> p a h c", c=65)[
                    :, st, 8 * ch:8 * ch + 8, 0:64]
                e.tensor_tensor(
                    dst, bank(pbank).rearrange("p (h c) -> p h c", c=64),
                    s_bb[:, ch * 512:(ch + 1) * 512].rearrange("p (h c) -> p h c",
                                                               c=64),
                    op=ALU.add).then_inc(sm["p2v"], 1)
    P2_V_DONE = C["p2v"]

    DBG_T = 0
    if dbg:
        dbg_d = {nm: nc.dram_tensor("dbg_" + nm, [128, 16384], BF16,
                                    kind="ExternalOutput").ap()
                 for nm in ["xlnT", "qT", "kT"]}
        dbg_v = nc.dram_tensor("dbg_v", [128, 16640], BF16,
                               kind="ExternalOutput").ap()
        wait("gpsimd", "p2v", P2_V_DONE)
        gdma(dbg_d["xlnT"][:, :], A1[:, :])
        gdma(dbg_d["qT"][:, :], A2[:, :])
        gdma(dbg_d["kT"][:, :], A3[:, :])
        DBG_T = gdma(dbg_v[:, :], A4[:, :])

    wait("scalar", "p2v", P2_V_DONE)
    SQ_SLOT = [s_sq[:, 0:2048], s_P[:, 0:2048]]
    slot_free_at = [0, 0]
    gi3 = 0
    for hp in range(HP):
        for j, src in ((0, qT), (1, kT)):
            wait("scalar", "stT", slot_free_at[j])
            _, sq_t = inc("stS")

            @on("scalar")
            def _(e, hp=hp, src=src, j=j):
                e.activation(SQ_SLOT[j], src[:, hp, :],
                             AF.Square).then_inc(sm["stS"], 1)
            for h in range(2):
                wait("tensor", "stS", sq_t)
                if C["stV"] >= 1:
                    wait("tensor", "stV", C["stV"])
                _, nt = inc("stT")

                @on("tensor")
                def _(e, h=h, j=j):
                    for ch in range(4):
                        ins = e.matmul(
                            psA[0:1, (2 + ch) * 512:(2 + ch) * 512 + 512],
                            s_eAB[:, h:h + 1],
                            SQ_SLOT[j][:, ch * 512:(ch + 1) * 512],
                            start=True, stop=True)
                    ins.then_inc(sm["stT"], 1)
                wait("vector", "stT", nt)
                _, ch_t = inc("vCh")
                _, nv = inc("stV")
                scc = 64 + 4 * (gi3 % 2) + 16
                dstc = 4 * hp + 2 * j + h

                @on("vector")
                def _(e, ch_t=ch_t, scc=scc, dstc=dstc):
                    for ch in range(4):
                        ins = e.tensor_reduce(
                            s_nrm[0:1, scc + ch:scc + ch + 1],
                            psA[0:1, (2 + ch) * 512:(2 + ch) * 512 + 512],
                            axis=AX.X, op=ALU.max)
                    ins.then_inc(sm["vCh"], 1)
                    e.wait_ge(sm["vCh"], ch_t)
                    e.tensor_reduce(s_nrm[0:1, dstc:dstc + 1],
                                    s_nrm[0:1, scc:scc + 4], axis=AX.X,
                                    op=ALU.max).then_inc(sm["stV"], 1)
                gi3 += 1
            slot_free_at[j] = C["stT"]
    wait("vector", "stV", C["stV"])
    _, nv = inc("stV")
    nrm3 = s_nrm[0:1, 0:32].rearrange("o (a b) -> o a b", b=4)

    @on("vector")
    def _(e):
        e.tensor_tensor(s_nrm[0:1, 32:48].rearrange("o (a b) -> o a b", b=2),
                        nrm3[:, :, 0:2], nrm3[:, :, 2:4],
                        op=ALU.mult).then_inc(sm["stV"], 1)
    wait("scalar", "stV", nv)
    _, s1 = inc("sCh")
    _, ns = inc("stS")

    @on("scalar")
    def _(e, s1=s1):
        e.activation(s_nrm[0:1, 48:64], s_nrm[0:1, 32:48],
                     AF.Ln).then_inc(sm["sCh"], 1)
        e.wait_ge(sm["sCh"], s1)
        e.activation(s_nrm[0:1, 32:48], s_nrm[0:1, 48:64], AF.Exp,
                     scale=0.5).then_inc(sm["stS"], 1)
    wait("vector", "stS", ns)
    _, nv1 = inc("vCh")
    _, nv = inc("stV")

    @on("vector")
    def _(e, nv1=nv1):
        e.tensor_reduce(s_nrm[0:1, 64:65], s_nrm[0:1, 32:48], axis=AX.X,
                        op=ALU.max).then_inc(sm["vCh"], 1)
        e.wait_ge(sm["vCh"], nv1)
        e.tensor_scalar(s_nrm[0:1, 65:66], s_nrm[0:1, 64:65], -1.0, None,
                        op0=ALU.mult).then_inc(sm["stV"], 1)
    wait("tensor", "stV", nv)
    _, nt = inc("stT")

    @on("tensor")
    def _(e):
        e.matmul(psA[:, 6 * 512:6 * 512 + 1], s_on1[:, :],
                 s_nrm[0:1, 65:66], start=True, stop=True).then_inc(sm["stT"], 1)
    wait("vector", "stT", nt)
    _, nv = inc("stV")

    @on("vector")
    def _(e):
        e.tensor_copy(s_bB[:, 0:1], psA[:, 6 * 512:6 * 512 + 1]
                      ).then_inc(sm["stV"], 1)
    ST_V_DONE = C["stV"]

    PvAB = [s_P[:, 1024 * j:1024 * (j + 1)] for j in range(4)]
    PvA = [PvAB[j][:, 0:512] for j in range(4)]
    PvB = [PvAB[j][:, 512:1024] for j in range(4)]
    psS2 = [psA[:, 0:1024], psA[:, 1024:2048], psA[:, 3072:4096]]
    psSA = [t[:, 0:512] for t in psS2]
    psSB = [t[:, 512:1024] for t in psS2]
    psO = [psA[0:65, 2048:2560], psA[0:65, 2560:3072]]

    wait("scalar", "stV", ST_V_DONE)
    wait("tensor", "p2v", P2_V_DONE)
    NIT = HP * 4 * 16
    IT = [(hp, qc, kt) for hp in range(HP) for qc in range(4) for kt in range(16)]
    sc_t = {}
    av_t = {}
    exp_t = {}
    n_evac = 0

    def emit_scores(i):
        hp, qc, kt = IT[i]
        if i >= 3:
            wait("tensor", "sM", exp_t[i - 3])
        _, t1 = inc("tM")
        sc_t[i] = t1

        @on("tensor")
        def _(e, hp=hp, qc=qc, kt=kt, b=i % 3):
            e.matmul(psSA[b], kT[0:64, hp, kt * 128:(kt + 1) * 128],
                     qT[0:64, hp, qc * 512:(qc + 1) * 512],
                     start=True, stop=True, tile_position=(0, 0))
            e.matmul(psSB[b], kT[64:128, hp, kt * 128:(kt + 1) * 128],
                     qT[64:128, hp, qc * 512:(qc + 1) * 512],
                     start=True, stop=True,
                     tile_position=(64, 0)).then_inc(sm["tM"], 1)

    def emit_av(j):
        global_unused = None
        hp, qc, kt = IT[j]
        if kt == 0 and n_evac >= 1:
            wait("tensor", "vM", n_evac)
        wait("tensor", "sM", exp_t[j])
        _, t1 = inc("tM")
        av_t[j] = t1

        @on("tensor")
        def _(e, hp=hp, kt=kt, b=j % 4):
            e.matmul(psO[0], v_aug[:, kt, 65 * (2 * hp):65 * (2 * hp) + 65],
                     PvA[b], start=(kt == 0), stop=(kt == 15),
                     skip_group_check=True)
            e.matmul(psO[1], v_aug[:, kt, 65 * (2 * hp + 1):65 * (2 * hp + 1) + 65],
                     PvB[b], start=(kt == 0), stop=(kt == 15),
                     skip_group_check=True).then_inc(sm["tM"], 1)

    def emit_evac(hp, qc):
        nonlocal_n = None
        wait("vector", "tM", C["tM"])
        if dbg and n_evac == 0:
            wait("vector", dG, DBG_T)
        slot = (n_evac * 2) % 4
        _, v_t = inc("vM")

        @on("vector")
        def _(e, hp=hp, qc=qc, slot=slot):
            e.tensor_copy(oTu[0:64, hp, qc * 512:(qc + 1) * 512], psO[0][0:64, :])
            e.tensor_copy(oTu[64:128, hp, qc * 512:(qc + 1) * 512],
                          psO[1][0:64, :])
            e.tensor_copy(s_sumsc[0:1, slot * 512:(slot + 1) * 512],
                          psO[0][64:65, :])
            e.tensor_copy(s_sumsc[0:1, (slot + 1) * 512:(slot + 2) * 512],
                          psO[1][64:65, :]).then_inc(sm["vM"], 1)
        wait("sync", "vM", n_evac + 1)
        for ab in range(2):
            hh = 2 * hp + ab
            ci = hh * 4 + qc
            sydma(sums_d[2 * ci:2 * ci + 2, :].rearrange("a b -> (a b)")
                  .rearrange("(o n) -> o n", o=1),
                  s_sumsc[0:1, (slot + ab) * 512:(slot + ab + 1) * 512])
        if n_evac >= 1:
            wait("vector", dSY, C["dSY"] - 32)

    for i in range(NIT + 1):
        if i < NIT:
            emit_scores(i)
        if i >= 1:
            j = i - 1
            emit_av(j)
            if IT[j][2] == 15:
                emit_evac(IT[j][0], IT[j][1])
                n_evac += 1
        if i < NIT:
            wait("scalar", "tM", sc_t[i])
            if i >= 4:
                wait("scalar", "tM", av_t[i - 4])
            inc("sM", 1)
            exp_t[i] = C["sM"]

            @on("scalar")
            def _(e, b=i % 3, pb=i % 4):
                e.activation(PvAB[pb], psS2[b], AF.Exp,
                             bias=s_bB[:, 0:1]).then_inc(sm["sM"], 1)
    P4_TM = C["tM"]

    wait("sync", dSY, C["dSY"])
    t_s2 = sydma(s_sum2, sums_d[:, :])
    wait("vector", dSY, t_s2)
    _, p5v_t = inc("p5v")

    @on("vector")
    def _(e):
        with nc.allow_low_precision(reason="softmax 1/sum stored bf16"):
            e.reciprocal(s_sum2r[:, :], s_sum2).then_inc(sm["p5v"], 1)
    wait("sync", "p5v", p5v_t)
    t_rd = sydma(recip_d[:, :], s_sum2r[:, :])
    wait("sync", dSY, t_rd)
    RECIP_T = []
    for hp in range(HP):
        r0 = 0 if hp < 4 else 64
        off = 2048 * (hp % 4)
        for ab in range(2):
            hh = 2 * hp + ab
            t = sydma(s_W[r0 + ab:r0 + ab + 1, off:off + 2048],
                      recip_d[8 * hh:8 * hh + 8, :].rearrange("a b -> (a b)")
                      .rearrange("(o n) -> o n", o=1))
        RECIP_T.append(t)

    base_p5v = C["p5v"]
    for hp in range(HP):
        r0 = 0 if hp < 4 else 64
        off = 2048 * (hp % 4)
        wait("tensor", dSY, RECIP_T[hp])
        for qc in range(4):
            g = hp * 4 + qc
            pb = 6 + (g % 2)
            if g >= 2:
                wait("tensor", "p5v", base_p5v + g - 1)
            _, t_t = inc("p5t")

            @on("tensor")
            def _(e, qc=qc, r0=r0, off=off, pb=pb):
                e.matmul(bank(pb), s_E2T[r0:r0 + 2, :],
                         s_W[r0:r0 + 2, off + qc * 512:off + (qc + 1) * 512],
                         start=True, stop=True).then_inc(sm["p5t"], 1)
            wait("vector", "p5t", t_t)
            _, v_t = inc("p5v")

            @on("vector")
            def _(e, hp=hp, qc=qc, pb=pb):
                e.tensor_tensor(oTn[0:64, hp, qc * 512:(qc + 1) * 512],
                                oTu[0:64, hp, qc * 512:(qc + 1) * 512],
                                bank(pb)[0:64, :], op=ALU.mult)
                e.tensor_tensor(oTn[64:128, hp, qc * 512:(qc + 1) * 512],
                                oTu[64:128, hp, qc * 512:(qc + 1) * 512],
                                bank(pb)[64:128, :], op=ALU.mult
                                ).then_inc(sm["p5v"], 1)
    P5_V_DONE = C["p5v"]
    P5_T_DONE = C["p5t"]

    t_bo = sydma(s_row, bias_d["bo"][:].rearrange("(o n) -> o n", o=1))
    wait("tensor", "p5v", P5_V_DONE)
    BO_BC_V = bias_bcast(t_bo)

    wait("vector", "p2v", BO_BC_V)
    rq_t6 = {st: C["rq"] + st + 1 for st in range(ST)}
    rs_t6 = {st: C["rs"] + st + 1 for st in range(ST)}
    ln_t6 = {st: C["sCh"] + st + 1 for st in range(ST)}
    C["rq"] += ST
    C["rs"] += ST
    C["sCh"] += ST
    for st in range(ST + 1):
        @on("scalar")
        def _(e, st=st):
            if st >= 1:
                p = 16 * ((st - 1) % 2)
                e.wait_ge(sm["sCh"], ln_t6[st - 1])
                e.activation(s_stat[:, p + 14:p + 15], s_stat[:, p + 15:p + 16],
                             AF.Exp, scale=-0.5).then_inc(sm["rs"], 1)
            if st < ST:
                p = 16 * (st % 2)
                e.wait_ge(sm["rq"], rq_t6[st])
                e.activation(s_stat[:, p + 15:p + 16], s_stat[:, p + 13:p + 14],
                             AF.Ln, bias=s_eps[:, :]).then_inc(sm["sCh"], 1)
    x1_dma = []
    xr_dma = []
    P6A0 = C["p6a"]
    TS6 = [2 * k for k in range(8)] + [2 * k + 1 for k in range(8)]
    for w in range(ST + 1):
        if w >= 1:
            wp = w - 1
            st = TS6[wp]
            p = 16 * (wp % 2)
            wait("vector", "rs", rs_t6[wp])
            wait("vector", dSY, xr_dma[wp])
            _, m1 = inc("vCh")
            _, m2 = inc("vCh")
            _, e_t = inc("p6e")

            @on("vector")
            def _(e, wp=wp, p=p, m1=m1, m2=m2):
                scr = s_scr[:, (wp % 2) * 1024:(wp % 2 + 1) * 1024]
                xt = s_x[:, (wp % 2) * 1024:(wp % 2 + 1) * 1024]
                e.tensor_scalar(s_stat[:, p + 15:p + 16], s_stat[:, p + 12:p + 13],
                                s_stat[:, p + 14:p + 15], -1.0,
                                op0=ALU.mult, op1=ALU.mult).then_inc(sm["vCh"], 1)
                e.wait_ge(sm["vCh"], m1)
                e.tensor_scalar(scr, scr, s_stat[:, p + 14:p + 15],
                                s_stat[:, p + 15:p + 16],
                                op0=ALU.mult, op1=ALU.add).then_inc(sm["vCh"], 1)
                e.wait_ge(sm["vCh"], m2)
                e.tensor_tensor(xt, scr, xt, op=ALU.add).then_inc(sm["p6e"], 1)
            wait("gpsimd", "p6e", e_t)
            dst = (x1a_d if st % 2 == 0 else x1b_d)[
                128 * (st // 2):128 * (st // 2) + 128, :]
            x1_dma.append(gdma(dst, s_x[:, (wp % 2) * 1024:(wp % 2 + 1) * 1024]))
            if wp == 7:
                @on("gpsimd")
                def _(e, tgt=x1_dma[7]):
                    e.wait_ge(dG, tgt)
                    with allow_ici_all_to_all():
                        e.collective_compute(
                            "AllToAll", ALU.bypass,
                            replica_groups=[list(range(8))],
                            ins=[x1a_d[:, :]], outs=[x1ga_d[:, :]]).then_inc(ccS)
        if w < ST:
            st = TS6[w]
            if w >= 2:
                wait("sync", "p6e", w - 1)
                wait("sync", dG, x1_dma[w - 2])
            xr_dma.append(sydma(s_x[:, (w % 2) * 1024:(w % 2 + 1) * 1024],
                                x_d[st * 128:(st + 1) * 128, :]))
            pb = 2 if w % 2 == 0 else 4
            if w >= 2:
                wait("tensor", "p6a", P6A0 + w - 1)
            _, t_t = inc("p6t")

            @on("tensor")
            def _(e, st=st, pb=pb):
                for ch in range(2):
                    for et in range(ET):
                        ins = e.matmul(bank(pb + ch),
                                       oTn[:, et, st * 128:(st + 1) * 128],
                                       woTb[:, et, ch * 512:(ch + 1) * 512],
                                       start=(et == 0), stop=(et == ET - 1))
                ins.then_inc(sm["p6t"], 1)
            wait("vector", "p6t", t_t)
            _, a_t = inc("p6a")
            _, sb_v = inc("vCh")

            @on("vector")
            def _(e, w=w, pb=pb, sb_v=sb_v, a_t=a_t):
                p = 16 * (w % 2)
                scr = s_scr[:, (w % 2) * 1024:(w % 2 + 1) * 1024]
                e.tensor_tensor(scr, bank2(pb), s_bb[:, :],
                                op=ALU.add).then_inc(sm["p6a"], 1)
                e.wait_ge(sm["p6a"], a_t)
                e.bn_stats(s_stat[:, p:p + 6], scr[:, 0:512])
                e.bn_stats(s_stat[:, p + 6:p + 12],
                           scr[:, 512:1024]).then_inc(sm["vCh"], 1)
                e.wait_ge(sm["vCh"], sb_v)
                e.bn_aggr(s_stat[:, p + 12:p + 14],
                          s_stat[:, p:p + 12]).then_inc(sm["rq"], 1)
    P6_T_DONE = C["p6t"]

    wait("gpsimd", dG, C["dG"])

    @on("gpsimd")
    def _(e):
        with allow_ici_all_to_all():
            e.collective_compute(
                "AllToAll", ALU.bypass, replica_groups=[list(range(8))],
                ins=[x1b_d[:, :]], outs=[x1gb_d[:, :]]).then_inc(ccS2)

    load_weight("fwcT", wcTb, queue=scdma, dep=("tM", P4_TM))
    load_weight("fucT", ucTb, queue=scdma)
    load_weight("fwiT", wiTb, queue=scdma)
    load_weight("fuiT", uiTb, queue=scdma)
    load_weight("fwfT", wfTb, queue=scdma, dep=("p5v", P5_V_DONE))
    load_weight("fufT", ufTb, queue=scdma)
    load_weight("fuoT", uoTb, queue=scdma, dep=("p5t", P5_T_DONE))
    load_weight("fwoT", woTb, queue=scdma, dep=("p6t", P6_T_DONE))
    FFN_CAST_DONE = C["vW"]

    wait("sync", ccS, 1)
    wait("sync", dG, x1_dma[-1])
    xg_dma = []
    for w in range(ST):
        if w == 8:
            wait("sync", ccS2, 1)
        if w >= 2:
            wait("sync", "p8a", w - 1)
        src = (x1ga_d if w < 8 else x1gb_d)[
            128 * (w % 8):128 * (w % 8) + 128, :]
        xg_dma.append(sydma(s_x[:, (w % 2) * 1024:(w % 2 + 1) * 1024], src))
    P8_COLS = [256 * (w % 8) + 128 * (w // 8) for w in range(ST)]
    ln_transpose_pipe(xlnFT, xg_dma, dSY, "p8a", "p8tr", "p8e", 0, 0,
                      cols=P8_COLS)

    wait("vector", dSY, FB_T)
    _, ci1 = inc("vCh")
    _, ci2 = inc("vCh")
    _, p9v_init = inc("p9v")

    @on("vector")
    def _(e, ci1=ci1, ci2=ci2):
        for et in range(ET):
            ins = e.memset(cT[:, et, :], 0.0)
        ins.then_inc(sm["vCh"], 1)
        e.wait_ge(sm["vCh"], ci1)
        for et in range(ET):
            ins = e.tensor_scalar(cT[:, et, :], cT[:, et, :],
                                  s_bp[:, BP["finc"] * 8 + et:
                                       BP["finc"] * 8 + et + 1],
                                  None, op0=ALU.add)
        ins.then_inc(sm["vCh"], 1)
        e.wait_ge(sm["vCh"], ci2)
        for et in range(ET):
            ins = e.tensor_copy(cTb[:, et, :], cT[:, et, :])
        ins.then_inc(sm["p9v"], 1)

    GFUNC = {"f": AF.Sigmoid, "i": AF.Sigmoid, "c": AF.Tanh}
    GW = {"f": (wfTb, ufTb, "fbf"), "i": (wiTb, uiTb, "fbi"),
          "c": (wcTb, ucTb, "fbc")}
    wait("tensor", "p8e", C["p8e"])
    wait("tensor", "vW", FFN_CAST_DONE)
    wait("tensor", "p9v", p9v_init)
    wait("scalar", dSY, FB_T)
    gidx = 0
    for t in range(NT):
        for ft in range(ET):
            for g in ["f", "i", "c"]:
                slot = gidx % 6
                if gidx >= 6:
                    wait("tensor", "p9s", gidx - 5)
                _, t_t = inc("p9t")

                @on("tensor")
                def _(e, t=t, ft=ft, g=g, slot=slot,
                      first=(ft == 0 and g == "f")):
                    wT, uT, _b = GW[g]
                    for et in range(ET):
                        e.matmul(bank(slot)[:, 0:256],
                                 wT[:, et, ft * 128:(ft + 1) * 128],
                                 xlnFT[:, et, t * 256:(t + 1) * 256],
                                 start=(et == 0), stop=False)
                    if first and t >= 1:
                        e.wait_ge(sm["p9v"], p9v_init + t)
                    for et in range(ET):
                        ins = e.matmul(bank(slot)[:, 0:256],
                                       uT[:, et, ft * 128:(ft + 1) * 128],
                                       cTb[:, et, :],
                                       start=False, stop=(et == ET - 1))
                    ins.then_inc(sm["p9t"], 1)
                wait("scalar", "p9t", t_t)
                bcol = BP[GW[g][2]] * 8 + ft
                _, s_t = inc("p9s")

                @on("scalar")
                def _(e, g=g, ft=ft, slot=slot, bcol=bcol):
                    e.activation(gate[g][:, ft, :], bank(slot)[:, 0:256],
                                 GFUNC[g], bias=s_bp[:, bcol:bcol + 1]
                                 ).then_inc(sm["p9s"], 1)
                gidx += 1
            wait("vector", "p9s", C["p9s"])
            _, c1 = inc("vCh")
            _, c2 = inc("vCh")

            @on("vector")
            def _(e, ft=ft, c1=c1, c2=c2):
                e.wait_ge(sm["vCh"], c1 - 1)
                e.tensor_tensor(g_tmp, gate["i"][:, ft, :], gate["c"][:, ft, :],
                                op=ALU.mult)
                e.tensor_tensor(s_tmpf, gate["f"][:, ft, :], cT[:, ft, :],
                                op=ALU.mult).then_inc(sm["vCh"], 1)
                e.wait_ge(sm["vCh"], c1)
                e.tensor_tensor(cT[:, ft, :], s_tmpf, g_tmp,
                                op=ALU.add).then_inc(sm["vCh"], 1)
        wait("vector", "p9t", C["p9t"])
        wait("vector", "vCh", C["vCh"])
        _, v_t = inc("p9v")

        @on("vector")
        def _(e):
            for ft in range(ET):
                ins = e.tensor_copy(cTb[:, ft, :], cT[:, ft, :])
            ins.then_inc(sm["p9v"], 1)
    P9_V_DONE = C["p9v"]

    wait("sync", "p8e", C["p8e"])
    t_bo2 = sydma(s_row, bias_d["fbo"][:].rearrange("(o n) -> o n", o=1))
    wait("tensor", "p9v", P9_V_DONE)
    BO2_BC_V = bias_bcast(t_bo2)

    wait("vector", "p2v", BO2_BC_V)
    U_A0 = C["p10a"]
    for pt in range(2):
        for ch in range(2):
            g = pt * 2 + ch
            pb = g % 2
            if g >= 2:
                wait("tensor", "p10a", U_A0 + g - 1)
            _, t_t = inc("p10t")

            @on("tensor")
            def _(e, pt=pt, ch=ch, pb=pb):
                for et in range(ET):
                    ins = e.matmul(bank(pb), cTb[:, et, pt * 128:(pt + 1) * 128],
                                   uoTb[:, et, ch * 512:(ch + 1) * 512],
                                   start=(et == 0), stop=(et == ET - 1))
                ins.then_inc(sm["p10t"], 1)
            wait("vector", "p10t", t_t)
            _, a_t = inc("p10a")

            @on("vector")
            def _(e, pt=pt, ch=ch, pb=pb):
                e.tensor_tensor(U_plus[:, pt, ch * 512:(ch + 1) * 512], bank(pb),
                                s_bb[:, ch * 512:(ch + 1) * 512],
                                op=ALU.add).then_inc(sm["p10a"], 1)
    U_A_DONE = C["p10a"]

    wait("sync", "p9v", P9_V_DONE)
    rq_ta = {i: C["rq"] + i + 1 for i in range(ST)}
    rs_ta = {i: C["rs"] + i + 1 for i in range(ST)}
    ln_ta = {i: C["sCh"] + i + 1 for i in range(ST)}
    C["rq"] += ST
    C["rs"] += ST
    C["sCh"] += ST
    tanh_t = {}
    add_t_hold = {}
    res_dma = []
    out_dma = []
    for i in range(ST + 2):
        if i < ST:
            _, h_t = inc("p10h")
            tanh_t[i] = h_t

        @on("scalar")
        def _(e, i=i):
            if i >= 2:
                p = 16 * ((i - 2) % 2)
                e.wait_ge(sm["sCh"], ln_ta[i - 2])
                e.activation(s_stat[:, p + 14:p + 15], s_stat[:, p + 15:p + 16],
                             AF.Exp, scale=-0.5).then_inc(sm["rs"], 1)
            if i >= 1 and i - 1 < ST:
                p = 16 * ((i - 1) % 2)
                e.wait_ge(sm["rq"], rq_ta[i - 1])
                e.activation(s_stat[:, p + 15:p + 16], s_stat[:, p + 13:p + 14],
                             AF.Ln, bias=s_eps[:, :]).then_inc(sm["sCh"], 1)
            if i < ST:
                e.wait_ge(sm["p10a"], add_t_hold[i])
                scr = s_scr[:, (i % 2) * 1024:(i % 2 + 1) * 1024]
                e.activation(scr, scr, AF.Tanh).then_inc(sm["p10h"], 1)
        if i >= 2:
            ip = i - 2
            p = 16 * (ip % 2)
            wait("vector", "rs", rs_ta[ip])
            wait("vector", dSY, res_dma[ip])
            _, m1 = inc("vCh")
            _, m2 = inc("vCh")
            _, e_t = inc("p10e")

            @on("vector")
            def _(e, ip=ip, p=p, m1=m1, m2=m2):
                scr = s_scr[:, (ip % 2) * 1024:(ip % 2 + 1) * 1024]
                xt = s_x[:, (ip % 2) * 1024:(ip % 2 + 1) * 1024]
                e.tensor_scalar(s_stat[:, p + 15:p + 16], s_stat[:, p + 12:p + 13],
                                s_stat[:, p + 14:p + 15], -1.0,
                                op0=ALU.mult, op1=ALU.mult).then_inc(sm["vCh"], 1)
                e.wait_ge(sm["vCh"], m1)
                e.tensor_scalar(scr, scr, s_stat[:, p + 14:p + 15],
                                s_stat[:, p + 15:p + 16],
                                op0=ALU.mult, op1=ALU.add).then_inc(sm["vCh"], 1)
                e.wait_ge(sm["vCh"], m2)
                e.tensor_tensor(xt, scr, xt, op=ALU.add).then_inc(sm["p10e"], 1)
            wait("gpsimd", "p10e", e_t)
            out_dma.append(gdma(out_d[ip // 2, (ip % 2) * 128:(ip % 2) * 128 + 128, :],
                                s_x[:, (ip % 2) * 1024:(ip % 2 + 1) * 1024]))
        if i >= 1 and i - 1 < ST:
            ib = i - 1
            p = 16 * (ib % 2)
            wait("vector", "p10h", tanh_t[ib])
            _, sb_v = inc("vCh")

            @on("vector")
            def _(e, ib=ib, p=p, sb_v=sb_v):
                scr = s_scr[:, (ib % 2) * 1024:(ib % 2 + 1) * 1024]
                e.bn_stats(s_stat[:, p:p + 6], scr[:, 0:512])
                e.bn_stats(s_stat[:, p + 6:p + 12],
                           scr[:, 512:1024]).then_inc(sm["vCh"], 1)
                e.wait_ge(sm["vCh"], sb_v)
                e.bn_aggr(s_stat[:, p + 12:p + 14],
                          s_stat[:, p:p + 12]).then_inc(sm["rq"], 1)
        if i < ST:
            if i >= 2:
                wait("sync", "p10e", i - 1)
                wait("sync", dG, out_dma[i - 2])
            rsrc = (x1ga_d if i % 2 == 0 else x1gb_d)[
                128 * (i // 2):128 * (i // 2) + 128, :]
            res_dma.append(sydma(s_x[:, (i % 2) * 1024:(i % 2 + 1) * 1024], rsrc))
            t, st2 = i // 2, i % 2
            pb = 2 if i % 2 == 0 else 4
            if i >= 2:
                wait("tensor", "p10a", U_A_DONE + i - 1)
            _, t_t = inc("p10t")

            @on("tensor")
            def _(e, t=t, st2=st2, pb=pb):
                for ch in range(2):
                    for et in range(ET):
                        ins = e.matmul(
                            bank(pb + ch),
                            xlnFT[:, et,
                                  t * 256 + st2 * 128:t * 256 + st2 * 128 + 128],
                            woTb[:, et, ch * 512:(ch + 1) * 512],
                            start=(et == 0), stop=(et == ET - 1))
                ins.then_inc(sm["p10t"], 1)
            wait("vector", "p10t", t_t)
            _, a_t = inc("p10a")
            add_t_hold[i] = a_t

            @on("vector")
            def _(e, i=i, st2=st2, pb=pb):
                scr = s_scr[:, (i % 2) * 1024:(i % 2 + 1) * 1024]
                e.tensor_tensor(scr, bank2(pb), U_plus[:, st2, :],
                                op=ALU.add).then_inc(sm["p10a"], 1)

    wait("gpsimd", dG, C["dG"])
    wait("sync", dSY, C["dSY"])
    wait("scalar", dS2, C["dS2"])

    with es, nc.Block() as blk:
        @blk.tensor
        def _(e):
            for f in ops["tensor"]:
                f(e)

        @blk.scalar
        def _(e):
            for f in ops["scalar"]:
                f(e)

        @blk.vector
        def _(e):
            for f in ops["vector"]:
                f(e)

        @blk.gpsimd
        def _(e):
            for f in ops["gpsimd"]:
                f(e)

        @blk.sync
        def _(e):
            for f in ops["sync"]:
                f(e)

    return nc


_NC_CACHE = None


def kernel(L0_attn_x, L0_attn_Wq, L0_attn_bq, L0_attn_Wk, L0_attn_bk,
           L0_attn_Wv, L0_attn_bv, L0_attn_Wo, L0_attn_bo,
           L0_ffn_wf, L0_ffn_uf, L0_ffn_bf, L0_ffn_wi, L0_ffn_ui, L0_ffn_bi,
           L0_ffn_wc, L0_ffn_uc, L0_ffn_bc, L0_ffn_wo, L0_ffn_uo, L0_ffn_bo,
           L0_ffn_in_c):
    global _NC_CACHE
    if _NC_CACHE is None:
        _NC_CACHE = build()
    nc = _NC_CACHE

    tr = lambda w: np.ascontiguousarray(np.asarray(w).T)
    f32c = lambda v: np.ascontiguousarray(np.asarray(v), dtype=np.float32)
    common = {
        "wqT": tr(L0_attn_Wq), "wkT": tr(L0_attn_Wk), "wvT": tr(L0_attn_Wv),
        "woT": tr(L0_attn_Wo),
        "fwfT": tr(L0_ffn_wf), "fufT": tr(L0_ffn_uf),
        "fwiT": tr(L0_ffn_wi), "fuiT": tr(L0_ffn_ui),
        "fwcT": tr(L0_ffn_wc), "fucT": tr(L0_ffn_uc),
        "fwoT": tr(L0_ffn_wo), "fuoT": tr(L0_ffn_uo),
        "b_bq": f32c(L0_attn_bq), "b_bk": f32c(L0_attn_bk),
        "b_bv": f32c(L0_attn_bv), "b_bo": f32c(L0_attn_bo),
        "b_fbf": f32c(L0_ffn_bf), "b_fbi": f32c(L0_ffn_bi),
        "b_fbc": f32c(L0_ffn_bc), "b_fbo": f32c(L0_ffn_bo),
        "b_finc": f32c(L0_ffn_in_c),
    }
    x = np.asarray(L0_attn_x)
    in_maps = [dict(common, x=np.ascontiguousarray(x[b])) for b in range(8)]
    res = run_bass_kernel_spmd(nc, in_maps, core_ids=list(range(8)))
    out = np.concatenate([res.results[p]["out"] for p in range(8)], axis=1)
    return out


# version 09
# speedup vs baseline: 1.0262x; 1.0262x over previous
import numpy as np
import concourse.bass as bass
import concourse.mybir as mybir
from concourse.bass_utils import run_bass_kernel_spmd
from concourse.aot_env import allow_ici_all_to_all
from contextlib import ExitStack

F32 = mybir.dt.float32
BF16 = mybir.dt.bfloat16
AF = mybir.ActivationFunctionType
ALU = mybir.AluOpType
AX = mybir.AxisListType

S, E, H, D = 2048, 1024, 16, 64
HP, ET, ST = H // 2, E // 128, S // 128
PC, NT = S // 8, 8
EPS = 1e-5
SM_SCALE = float(1.0 / np.sqrt(D))

ATTN_W = ["wqT", "wkT", "wvT", "woT"]
FFN_W = ["fwcT", "fucT", "fwiT", "fuiT", "fwfT", "fufT", "fwoT", "fuoT"]
BIAS_NAMES = ["bq", "bk", "bv", "bo", "fbf", "fbi", "fbc", "fbo", "finc"]
BP = {"bq": 0, "bk": 1, "fbf": 2, "fbi": 3, "fbc": 4, "finc": 5}


def build(dbg=False):
    nc = bass.Bass("TRN2", target_bir_lowering=False, debug=False)

    x_d = nc.dram_tensor("x", [S, E], F32, kind="ExternalInput").ap()
    wT_d = {nm: nc.dram_tensor(nm, [E, E], F32, kind="ExternalInput").ap()
            for nm in ATTN_W + FFN_W}
    bias_d = {nm: nc.dram_tensor("b_" + nm, [E], F32, kind="ExternalInput").ap()
              for nm in BIAS_NAMES}
    out_d = nc.dram_tensor("out", [NT, PC, E], F32, kind="ExternalOutput").ap()
    x1_d = nc.dram_tensor("x1b", [S, E], F32).ap()
    x1g_d = nc.dram_tensor("x1g", [S, E], F32).ap()
    sums_d = nc.dram_tensor("sumsd", [128, 256], F32).ap()
    recip_d = nc.dram_tensor("recipd", [128, 256], BF16).ap()

    es = ExitStack()
    sb = lambda name, shape, dt: es.enter_context(nc.sbuf_tensor(name, shape, dt))
    sem = lambda name: es.enter_context(nc.semaphore(name))

    s_x = sb("s_x", [128, 2048], F32)
    s_scr = sb("s_scr", [128, 2048], F32)
    A1 = sb("A1", [128, 16384], BF16)
    A2 = sb("A2", [128, 16384], BF16)
    A3 = sb("A3", [128, 16384], BF16)
    A4 = sb("A4", [128, 16640], BF16)
    s_W = sb("s_W", [128, 16384], BF16)
    s_wst = sb("s_wst", [128, 2048], F32)
    s_P = sb("s_P", [128, 4096], BF16)
    s_sq = sb("s_sq", [128, 2048], BF16)
    s_cb = sb("s_cb", [128, 2048], BF16)
    s_sum2r = sb("s_sum2r", [128, 256], BF16)
    s_bp = sb("s_bp", [128, 48], F32)
    s_bB = sb("s_bB", [128, 16], F32)
    s_stat = sb("s_stat", [128, 32], F32)
    s_id = sb("s_id", [128, 128], F32)
    s_eAB = sb("s_eAB", [128, 2], BF16)
    s_E2T = sb("s_E2T", [128, 128], BF16)
    s_on1 = sb("s_on1", [1, 128], F32)
    s_nrm = sb("s_nrm", [1, 96], F32)
    s_eps = sb("s_eps", [128, 1], F32)
    s_sumsc = s_scr[0:1, 0:2048]
    s_sum2 = s_scr[:, 0:256]
    s_bb = s_P[:, 0:1024]
    s_tmpf = s_wst[:, 0:256]
    s_row = s_scr[0:1, 0:1024]
    g_tmp = s_sum2r[:, 0:256]

    r3 = lambda t, w: t.rearrange("p (a b) -> p a b", b=w)
    xlnT = r3(A1[:, :], 2048)
    oTu = r3(A1[:, :], 2048)
    wfTb, ufTb = r3(A1[:, 0:8192], 1024), r3(A1[:, 8192:16384], 1024)
    qT = r3(A2[:, :], 2048)
    oTn = r3(A2[:, :], 2048)
    xlnFT = r3(A2[:, :], 2048)
    kT = r3(A3[:, :], 2048)
    wiTb, uiTb = r3(A3[:, 0:8192], 1024), r3(A3[:, 8192:16384], 1024)
    v_aug = r3(A4[:, :], 1040)
    wcTb, ucTb = r3(A4[:, 0:8192], 1024), r3(A4[:, 8192:16384], 1024)
    sWt = [r3(s_W[:, 0:8192], 1024), r3(s_W[:, 8192:16384], 1024)]
    uoTb, woTb = sWt[0], sWt[1]
    cT = r3(s_x[:, :], 256)
    cTb = r3(s_cb[:, :], 256)
    gate = {"f": r3(s_P[:, 0:2048], 256), "i": r3(s_P[:, 2048:4096], 256),
            "c": r3(s_sq[:, 0:2048], 256)}
    U_plus = r3(s_wst[:, :], 1024)

    psA = es.enter_context(nc.psum_tensor("psA", [128, 4096], F32))
    bank = lambda b: psA[:, 512 * b:512 * (b + 1)]
    bank2 = lambda b: psA[:, 512 * b:512 * (b + 2)]

    dG, dSY, ccS, dS2 = sem("dG"), sem("dSY"), sem("ccS"), sem("dS2")
    SEMS = ["gI", "vCh", "sCh", "gCh", "rq", "rs",
            "p1a", "p1tr", "p1e", "vW", "p2t", "p2v",
            "stS", "stT", "stV", "tM", "sM", "vM", "p5t", "p5v",
            "p6t", "p6a", "p6e", "p8a", "p8tr", "p8e",
            "p9t", "p9s", "p9v", "p10t", "p10a", "p10h", "p10e"]
    sm = {n: sem(n) for n in SEMS}
    C = {n: 0 for n in SEMS}
    C["dG"] = 0
    C["dSY"] = 0
    C["dS2"] = 0

    ops = {k: [] for k in ["tensor", "scalar", "vector", "gpsimd", "sync"]}
    ENG_OF = {"tensor": "tensor", "scalar": "scalar", "vector": "vector",
              "gpsimd": "gpsimd", "sync": "sync"}

    def on(eng):
        def deco(f):
            ops[eng].append(f)
        return deco

    def wait(eng, s, v):
        h = sm[s] if isinstance(s, str) else s

        @on(eng)
        def _(e, h=h, v=v):
            e.wait_ge(h, v)

    def inc(s, n=1):
        C[s] += n
        return sm[s], C[s]

    def gdma(out, in_):
        prev = C["dG"]
        C["dG"] += 16
        t = C["dG"]

        @on("gpsimd")
        def _(e, out=out, in_=in_, prev=prev):
            if prev:
                e.wait_ge(dG, prev)
            e.dma_start(out=out, in_=in_).then_inc(dG, 16)
        return t

    def sydma(out, in_):
        prev = C["dSY"]
        C["dSY"] += 16
        t = C["dSY"]

        @on("sync")
        def _(e, out=out, in_=in_, prev=prev):
            if prev:
                e.wait_ge(dSY, prev)
            e.dma_start(out=out, in_=in_).then_inc(dSY, 16)
        return t

    def scdma(out, in_):
        prev = C["dS2"]
        C["dS2"] += 16
        t = C["dS2"]

        @on("scalar")
        def _(e, out=out, in_=in_, prev=prev):
            if prev:
                e.wait_ge(dS2, prev)
            e.dma_start(out=out, in_=in_).then_inc(dS2, 16)
        return t

    h_gCh, v_gCh1 = inc("gCh")

    @on("gpsimd")
    def _(e, tgt=v_gCh1):
        e.memset(s_id[:, :], 0.0)
        e.memset(s_on1[:, :], 1.0)
        e.memset(s_eps[:, :], EPS)
        e.memset(v_aug.rearrange("p a (h c) -> p a h c", c=65)[:, :, :, 64:65],
                 1.0).then_inc(sm["gCh"], 1)
        e.wait_ge(sm["gCh"], tgt)
        e.affine_select(out=s_id[:, :], in_=s_id[:, :], compare_op=ALU.not_equal,
                        fill=1.0, base=0, pattern=[[-1, 128]], channel_multiplier=1)
        e.iota(s_stat[:, 0:1], pattern=[[0, 1]], base=0, channel_multiplier=1,
               allow_small_or_imprecise_dtypes=True)
        e.iota(s_scr[0:2, 0:128], pattern=[[1, 128]], base=0,
               channel_multiplier=-64,
               allow_small_or_imprecise_dtypes=True).then_inc(sm["gI"], 1)
    C["gI"] += 1
    wait("vector", "gI", 1)
    _, v_e2a = inc("vCh")
    _, v_e2b = inc("vCh")

    @on("vector")
    def _(e, v_e2a=v_e2a, v_e2b=v_e2b):
        e.tensor_scalar(s_eAB[:, 0:1], s_stat[:, 0:1], 64.0, None, op0=ALU.is_lt)
        e.tensor_scalar(s_eAB[:, 1:2], s_stat[:, 0:1], 64.0, None, op0=ALU.is_ge)
        e.tensor_scalar(s_scr[0:2, 128:256], s_scr[0:2, 0:128], 0.0, None,
                        op0=ALU.is_ge)
        e.tensor_scalar(s_scr[0:2, 256:384], s_scr[0:2, 0:128], 64.0, None,
                        op0=ALU.is_lt).then_inc(sm["vCh"], 1)
        e.wait_ge(sm["vCh"], v_e2a)
        e.tensor_tensor(s_E2T[0:2, :], s_scr[0:2, 128:256], s_scr[0:2, 256:384],
                        op=ALU.mult).then_inc(sm["vCh"], 1)
        e.wait_ge(sm["vCh"], v_e2b)
        e.tensor_copy(s_E2T[64:66, :], s_E2T[0:2, :]).then_inc(sm["vCh"], 1)
    inc("vCh")

    wcast_done = {}
    W_CHUNK = [0]

    def load_weight(name, slot_or_dst, queue=None, dep=None):
        q = queue or sydma
        qeng = {"sydma": "sync", "gdma": "gpsimd", "scdma": "scalar"}[
            "sydma" if q is sydma else ("gdma" if q is gdma else "scdma")]
        dsem = dSY if q is sydma else (dG if q is gdma else dS2)
        dst = sWt[slot_or_dst] if isinstance(slot_or_dst, int) else slot_or_dst
        for et in range(ET):
            g = W_CHUNK[0]
            if g >= 2:
                wait(qeng, "vW", g - 1)
            if dep is not None and et == 0:
                wait(qeng, dep[0], dep[1])
            t = q(s_wst[:, (g % 2) * 1024:(g % 2 + 1) * 1024],
                  wT_d[name][et * 128:(et + 1) * 128, :])
            wait("vector", dsem, t)
            inc("vW")

            @on("vector")
            def _(e, et=et, g=g, dst=dst):
                e.tensor_copy(dst[:, et, :],
                              s_wst[:, (g % 2) * 1024:(g % 2 + 1) * 1024]
                              ).then_inc(sm["vW"], 1)
            W_CHUNK[0] += 1
        wcast_done[name] = C["vW"]

    load_weight("wqT", 0)

    def ln_chain(src_ap, dst_ap, apply_sem):
        _, t1 = inc("vCh")

        @on("vector")
        def _(e, src=src_ap, t1=t1):
            e.bn_stats(s_stat[:, 0:6], src[:, 0:512])
            e.bn_stats(s_stat[:, 6:12], src[:, 512:1024]).then_inc(sm["vCh"], 1)
            e.wait_ge(sm["vCh"], t1)
            e.bn_aggr(s_stat[:, 12:14], s_stat[:, 0:12]).then_inc(sm["rq"], 1)
        _, rq_t = inc("rq")
        wait("scalar", "rq", rq_t)
        _, s1 = inc("sCh")

        @on("scalar")
        def _(e, s1=s1):
            e.activation(s_stat[:, 15:16], s_stat[:, 13:14], AF.Ln,
                         bias=s_eps[:, :]).then_inc(sm["sCh"], 1)
            e.wait_ge(sm["sCh"], s1)
            e.activation(s_stat[:, 14:15], s_stat[:, 15:16], AF.Exp,
                         scale=-0.5).then_inc(sm["rs"], 1)
        _, rs_t = inc("rs")
        wait("vector", "rs", rs_t)
        _, t2 = inc("vCh")
        _, ap_t = inc(apply_sem)

        @on("vector")
        def _(e, src=src_ap, dst=dst_ap, t2=t2, apply_sem=apply_sem):
            e.tensor_scalar(s_stat[:, 15:16], s_stat[:, 12:13], s_stat[:, 14:15],
                            -1.0, op0=ALU.mult, op1=ALU.mult).then_inc(sm["vCh"], 1)
            e.wait_ge(sm["vCh"], t2)
            e.tensor_scalar(dst, src, s_stat[:, 14:15], s_stat[:, 15:16],
                            op0=ALU.mult, op1=ALU.add).then_inc(sm[apply_sem], 1)
        return ap_t

    def ln_transpose_pipe(dstT, dma_tgts, dma_sem, ap_sem, tr_sem, ev_sem,
                          base_tr, base_ev):
        rq_t = {st: C["rq"] + st + 1 for st in range(ST)}
        rs_t = {st: C["rs"] + st + 1 for st in range(ST)}
        ln_t = {st: C["sCh"] + st + 1 for st in range(ST)}
        C["rq"] += ST
        C["rs"] += ST
        C["sCh"] += ST
        for st in range(ST + 1):
            @on("scalar")
            def _(e, st=st):
                if st >= 1:
                    p = 16 * ((st - 1) % 2)
                    e.wait_ge(sm["sCh"], ln_t[st - 1])
                    e.activation(s_stat[:, p + 14:p + 15], s_stat[:, p + 15:p + 16],
                                 AF.Exp, scale=-0.5).then_inc(sm["rs"], 1)
                if st < ST:
                    p = 16 * (st % 2)
                    e.wait_ge(sm["rq"], rq_t[st])
                    e.activation(s_stat[:, p + 15:p + 16], s_stat[:, p + 13:p + 14],
                                 AF.Ln, bias=s_eps[:, :]).then_inc(sm["sCh"], 1)
        for st in range(ST + 1):
            if st >= 1:
                stp = st - 1
                p = 16 * (stp % 2)
                wait("vector", "rs", rs_t[stp])
                if stp >= 2:
                    wait("vector", tr_sem, base_tr + 2 * (stp - 2) + 2)
                _, m1 = inc("vCh")
                _, ap_v = inc(ap_sem)

                @on("vector")
                def _(e, stp=stp, p=p, m1=m1, ap_sem=ap_sem):
                    e.tensor_scalar(s_stat[:, p + 15:p + 16], s_stat[:, p + 12:p + 13],
                                    s_stat[:, p + 14:p + 15], -1.0,
                                    op0=ALU.mult, op1=ALU.mult).then_inc(sm["vCh"], 1)
                    e.wait_ge(sm["vCh"], m1)
                    e.tensor_scalar(s_scr[:, (stp % 2) * 1024:(stp % 2 + 1) * 1024],
                                    s_x[:, (stp % 2) * 1024:(stp % 2 + 1) * 1024],
                                    s_stat[:, p + 14:p + 15],
                                    s_stat[:, p + 15:p + 16],
                                    op0=ALU.mult, op1=ALU.add
                                    ).then_inc(sm[ap_sem], 1)
                wait("tensor", ap_sem, ap_v)
                for g in range(2):
                    if stp >= 1:
                        wait("tensor", ev_sem, base_ev + 2 * (stp - 1) + g + 1)
                    _, tr_v = inc(tr_sem)

                    @on("tensor")
                    def _(e, stp=stp, g=g, tr_sem=tr_sem):
                        xln = s_scr[:, (stp % 2) * 1024:(stp % 2 + 1) * 1024]
                        for et in range(4 * g, 4 * g + 4):
                            ins = e.transpose(psA[:, g * 512 + (et % 4) * 128:
                                                  g * 512 + (et % 4) * 128 + 128],
                                              xln[:, et * 128:(et + 1) * 128],
                                              s_id[:, :])
                        ins.then_inc(sm[tr_sem], 1)
            if st < ST:
                wait("vector", dma_sem, dma_tgts[st])
                _, sb_v = inc("vCh")

                @on("vector")
                def _(e, st=st, sb_v=sb_v):
                    p = 16 * (st % 2)
                    xt = s_x[:, (st % 2) * 1024:(st % 2 + 1) * 1024]
                    e.bn_stats(s_stat[:, p:p + 6], xt[:, 0:512])
                    e.bn_stats(s_stat[:, p + 6:p + 12],
                               xt[:, 512:1024]).then_inc(sm["vCh"], 1)
                    e.wait_ge(sm["vCh"], sb_v)
                    e.bn_aggr(s_stat[:, p + 12:p + 14],
                              s_stat[:, p:p + 12]).then_inc(sm["rq"], 1)
            if st >= 1:
                stp = st - 1
                for g in range(2):
                    wait("vector", tr_sem, base_tr + 2 * stp + g + 1)
                    _, ev_v = inc(ev_sem)

                    @on("vector")
                    def _(e, stp=stp, g=g, dstT=dstT, ev_sem=ev_sem):
                        for et in range(4 * g, 4 * g + 4):
                            ins = e.tensor_copy(
                                dstT[:, et, stp * 128:(stp + 1) * 128],
                                psA[:, g * 512 + (et % 4) * 128:
                                    g * 512 + (et % 4) * 128 + 128])
                        ins.then_inc(sm[ev_sem], 1)

    xdma = []
    for st in range(ST):
        if st >= 2:
            wait("gpsimd", "p1a", st - 1)
        xdma.append(gdma(s_x[:, (st % 2) * 1024:(st % 2 + 1) * 1024],
                         x_d[st * 128:(st + 1) * 128, :]))
    ln_transpose_pipe(xlnT, xdma, dG, "p1a", "p1tr", "p1e", 0, 0)

    load_weight("wvT", 1)
    for nm in ["bq", "bk"]:
        for j in range(ET):
            sydma(s_bp[:, BP[nm] * 8 + j:BP[nm] * 8 + j + 1],
                  bias_d[nm][j * 128:(j + 1) * 128].rearrange("(p o) -> p o", o=1))
    BQBK_T = C["dSY"]
    for nm in ["fbf", "fbi", "fbc", "finc"]:
        for j in range(ET):
            sydma(s_bp[:, BP[nm] * 8 + j:BP[nm] * 8 + j + 1],
                  bias_d[nm][j * 128:(j + 1) * 128].rearrange("(p o) -> p o", o=1))
    FB_T = C["dSY"]

    b67 = {6: ("p2v", 0), 7: ("p2v", 0)}
    b67_next = [6]

    def take67(consumer):
        b = b67_next[0]
        b67_next[0] = 13 - b
        wait("tensor", *b67[b])
        b67[b] = consumer
        return b

    def bias_bcast(row_t):
        wait("tensor", dSY, row_t)
        _, v_t = inc("p2v")
        bA = take67(("p2v", v_t))
        bB = take67(("p2v", v_t))
        _, t_t = inc("p2t")

        @on("tensor")
        def _(e, bA=bA, bB=bB):
            e.matmul(bank(bA), s_on1[:, :], s_row[0:1, 0:512], start=True, stop=True)
            e.matmul(bank(bB), s_on1[:, :], s_row[0:1, 512:1024],
                     start=True, stop=True).then_inc(sm["p2t"], 1)
        wait("vector", "p2t", t_t)

        @on("vector")
        def _(e, bA=bA, bB=bB):
            e.tensor_copy(s_bb[:, 0:512], bank(bA))
            e.tensor_copy(s_bb[:, 512:1024], bank(bB)).then_inc(sm["p2v"], 1)
        return v_t

    qk_part = {}

    def emit_proj_group(hp, which, ch):
        slot = 0 if which == "q" else 1
        wname = "wqT" if which == "q" else "wkT"
        wait("tensor", "vW", wcast_done[wname])
        _, v_t = inc("p2v")
        b = take67(("p2v", v_t))
        _, t_t = inc("p2t")

        @on("tensor")
        def _(e, hp=hp, ch=ch, slot=slot, b=b):
            for et in range(ET):
                ins = e.matmul(bank(b), sWt[slot][:, et, hp * 128:(hp + 1) * 128],
                               xlnT[:, et, ch * 512:(ch + 1) * 512],
                               start=(et == 0), stop=(et == ET - 1))
            ins.then_inc(sm["p2t"], 1)
        wait("vector", "p2t", t_t)
        bcol = BP["bq" if which == "q" else "bk"] * 8 + hp
        dst = qT if which == "q" else kT
        sc2 = SM_SCALE if which == "q" else None

        @on("vector")
        def _(e, hp=hp, ch=ch, dst=dst, bcol=bcol, sc2=sc2, b=b):
            if sc2 is None:
                e.tensor_scalar(dst[:, hp, ch * 512:(ch + 1) * 512], bank(b),
                                s_bp[:, bcol:bcol + 1], None,
                                op0=ALU.add).then_inc(sm["p2v"], 1)
            else:
                e.tensor_scalar(dst[:, hp, ch * 512:(ch + 1) * 512], bank(b),
                                s_bp[:, bcol:bcol + 1], sc2,
                                op0=ALU.add, op1=ALU.mult).then_inc(sm["p2v"], 1)

    stab_sq = {}
    stab_nq = {}
    stab_nk = {}
    stab_done = {}
    tail_nt = {}

    def emit_stab_squares(hp, which):
        src = qT if which == "q" else kT
        wait("vector", "p2v", qk_part[(hp, which)])
        if which == "q":
            wait("vector", "stT", stab_nk.get(hp - 1, 0))
        else:
            wait("vector", "stT", stab_nq[hp])
        _, v_t = inc("stV")
        stab_sq[(hp, which)] = v_t

        @on("vector")
        def _(e, hp=hp, src=src):
            e.tensor_tensor(s_sq[:, 0:2048], src[:, hp, :], src[:, hp, :],
                            op=ALU.mult).then_inc(sm["stV"], 1)

    def emit_stab_norms(hp, which):
        sq_v = stab_sq[(hp, which)]
        outc = 0 if which == "q" else 2
        for h in range(2):
            _, r_t = inc("stV")
            b = take67(("stV", r_t))
            wait("tensor", "stV", sq_v)
            _, n_t = inc("stT")

            @on("tensor")
            def _(e, h=h, b=b):
                for ch in range(4):
                    ins = e.matmul(psA[0:1, b * 512:(b + 1) * 512],
                                   s_eAB[:, h:h + 1],
                                   s_sq[:, ch * 512:(ch + 1) * 512],
                                   start=(ch == 0), stop=(ch == 3))
                ins.then_inc(sm["stT"], 1)
            wait("vector", "stT", n_t)
            if h == 0 and outc == 0:
                wait("vector", "stV", stab_done.get(hp - 1, 0))

            @on("vector")
            def _(e, h=h, b=b, outc=outc):
                e.tensor_reduce(s_nrm[0:1, outc + h:outc + h + 1],
                                psA[0:1, b * 512:(b + 1) * 512], axis=AX.X,
                                op=ALU.max).then_inc(sm["stV"], 1)
        if which == "q":
            stab_nq[hp] = C["stT"]
        else:
            stab_nk[hp] = C["stT"]

    def emit_stab_tail(hp):
        nk_v = C["stV"]
        wait("vector", "stV", nk_v)
        _, v1 = inc("vCh")
        _, nv = inc("stV")

        @on("vector")
        def _(e, v1=v1):
            e.tensor_tensor(s_nrm[0:1, 4:6], s_nrm[0:1, 0:2], s_nrm[0:1, 2:4],
                            op=ALU.mult).then_inc(sm["vCh"], 1)
            e.wait_ge(sm["vCh"], v1)
            e.tensor_reduce(s_nrm[0:1, 6:7], s_nrm[0:1, 4:6], axis=AX.X,
                            op=ALU.max).then_inc(sm["stV"], 1)
        wait("scalar", "stV", nv)
        _, s1 = inc("sCh")
        _, ns = inc("stS")

        @on("scalar")
        def _(e, s1=s1):
            e.activation(s_nrm[0:1, 7:8], s_nrm[0:1, 6:7],
                         AF.Ln).then_inc(sm["sCh"], 1)
            e.wait_ge(sm["sCh"], s1)
            e.activation(s_nrm[0:1, 8:9], s_nrm[0:1, 7:8], AF.Exp,
                         scale=0.5).then_inc(sm["stS"], 1)
        wait("vector", "stS", ns)
        _, nv2 = inc("stV")

        @on("vector")
        def _(e):
            e.tensor_scalar(s_nrm[0:1, 9:10], s_nrm[0:1, 8:9], -1.0, None,
                            op0=ALU.mult).then_inc(sm["stV"], 1)
        _, cp_t = inc("stV")
        b = take67(("stV", cp_t))
        wait("tensor", "stV", nv2)
        _, nt = inc("stT")
        tail_nt[hp] = nt

        @on("tensor")
        def _(e, b=b):
            e.matmul(psA[:, b * 512:b * 512 + 1], s_on1[:, :],
                     s_nrm[0:1, 9:10], start=True, stop=True).then_inc(sm["stT"], 1)
        wait("vector", "stT", nt)

        @on("vector")
        def _(e, hp=hp, b=b):
            e.tensor_copy(s_bB[:, hp:hp + 1], psA[:, b * 512:b * 512 + 1]
                          ).then_inc(sm["stV"], 1)
        stab_done[hp] = C["stV"]

    wait("vector", dSY, BQBK_T)
    for ch in range(4):
        wait("tensor", "p1e", 8 * (ch + 1))
        emit_proj_group(0, "q", ch)
    qk_part[(0, "q")] = C["p2v"]

    wait("sync", "p1e", 32)
    BV_ROW_T = sydma(s_row, bias_d["bv"][:].rearrange("(o n) -> o n", o=1))
    BV_BC_V = bias_bcast(BV_ROW_T)

    wait("tensor", "vW", wcast_done["wvT"])
    EVAC_V = []
    for st in range(ST):
        for ch in range(2):
            pj = len(EVAC_V)
            pbank = 2 + (pj % 4)
            if pj >= 4:
                wait("tensor", "p2v", EVAC_V[pj - 4])
            _, t_t = inc("p2t")

            @on("tensor")
            def _(e, st=st, ch=ch, pbank=pbank):
                for et in range(ET):
                    ins = e.matmul(bank(pbank), xlnT[:, et, st * 128:(st + 1) * 128],
                                   sWt[1][:, et, ch * 512:(ch + 1) * 512],
                                   start=(et == 0), stop=(et == ET - 1))
                ins.then_inc(sm["p2t"], 1)
            wait("vector", "p2t", t_t)
            if pj == 0:
                wait("vector", "p2v", BV_BC_V)
            _, v_t = inc("p2v")
            EVAC_V.append(v_t)

            @on("vector")
            def _(e, st=st, ch=ch, pbank=pbank):
                dst = v_aug.rearrange("p a (h c) -> p a h c", c=65)[
                    :, st, 8 * ch:8 * ch + 8, 0:64]
                e.tensor_tensor(
                    dst, bank(pbank).rearrange("p (h c) -> p h c", c=64),
                    s_bb[:, ch * 512:(ch + 1) * 512].rearrange("p (h c) -> p h c",
                                                               c=64),
                    op=ALU.add).then_inc(sm["p2v"], 1)
    V_DONE_T = C["p2t"]
    load_weight("wkT", 1, dep=("p2t", V_DONE_T))

    for ch in range(4):
        emit_proj_group(0, "k", ch)
    qk_part[(0, "k")] = C["p2v"]
    emit_stab_squares(0, "q")
    emit_stab_norms(0, "q")
    emit_stab_squares(0, "k")
    emit_stab_norms(0, "k")
    emit_stab_tail(0)

    mult_done = {-1: 0}
    rb_emit = {}
    recip_rt = {}
    sc_t = {}
    av_t = {}
    exp_t = {}
    pair_sums_t = {}

    def emit_pair_recip_chain(hp):
        wait("sync", dSY, pair_sums_t[hp])
        if hp >= 1:
            wait("sync", "p5v", recip_rt[hp - 1])
        g_t = sydma(s_scr[32:48, 0:256], sums_d[16 * hp:16 * hp + 16, :])
        wait("vector", dSY, g_t)
        _, r_t = inc("p5v")
        recip_rt[hp] = r_t

        @on("vector")
        def _(e):
            with nc.allow_low_precision(reason="softmax 1/sum stored bf16"):
                e.reciprocal(s_sum2r[32:48, :], s_scr[32:48, 0:256]
                             ).then_inc(sm["p5v"], 1)
        wait("sync", "p5v", r_t)
        sydma(recip_d[16 * hp:16 * hp + 16, :], s_sum2r[32:48, :])
        wait("sync", "tM", sc_t[64 * hp + 63])
        rb = 0
        for ab in range(2):
            rb = sydma(
                A2[ab:ab + 1, 2048 * hp:2048 * hp + 2048],
                recip_d[16 * hp + 8 * ab:16 * hp + 8 * ab + 8, :]
                .rearrange("a b -> (a b)").rearrange("(o n) -> o n", o=1))
        rb_emit[hp] = rb

    def emit_rb_mults(hp):
        wait("tensor", dSY, rb_emit[hp])
        for qc in range(4):
            _, m_t = inc("p5v")
            b = take67(("p5v", m_t))
            _, t_t = inc("p5t")

            @on("tensor")
            def _(e, hp=hp, qc=qc, b=b):
                e.matmul(bank(b), s_E2T[0:2, :],
                         A2[0:2, 2048 * hp + qc * 512:2048 * hp + (qc + 1) * 512],
                         start=True, stop=True).then_inc(sm["p5t"], 1)
            wait("vector", "p5t", t_t)

            @on("vector")
            def _(e, hp=hp, qc=qc, b=b):
                e.tensor_tensor(oTn[0:64, hp, qc * 512:(qc + 1) * 512],
                                s_cb[0:64, qc * 512:(qc + 1) * 512],
                                bank(b)[0:64, :], op=ALU.mult)
                e.tensor_tensor(oTn[64:128, hp, qc * 512:(qc + 1) * 512],
                                s_cb[64:128, qc * 512:(qc + 1) * 512],
                                bank(b)[64:128, :], op=ALU.mult
                                ).then_inc(sm["p5v"], 1)
        mult_done[hp] = C["p5v"]

    PvAB = [s_P[:, 1024 * j:1024 * (j + 1)] for j in range(4)]
    PvA = [PvAB[j][:, 0:512] for j in range(4)]
    PvB = [PvAB[j][:, 512:1024] for j in range(4)]
    psS2 = [psA[:, 0:1024], psA[:, 1024:2048]]
    psSA = [t[:, 0:512] for t in psS2]
    psSB = [t[:, 512:1024] for t in psS2]
    psO = [psA[0:65, 2048:2560], psA[0:65, 2560:3072]]

    NIT = HP * 4 * 16
    IT = [(hp, qc, kt) for hp in range(HP) for qc in range(4) for kt in range(16)]
    n_evac = 0

    def emit_scores(i):
        hp, qc, kt = IT[i]
        if i >= 2:
            wait("tensor", "sM", exp_t[i - 2])
        _, t1 = inc("tM")
        sc_t[i] = t1

        @on("tensor")
        def _(e, hp=hp, qc=qc, kt=kt, b=i % 2):
            e.matmul(psSA[b], kT[0:64, hp, kt * 128:(kt + 1) * 128],
                     qT[0:64, hp, qc * 512:(qc + 1) * 512],
                     start=True, stop=True, tile_position=(0, 0))
            e.matmul(psSB[b], kT[64:128, hp, kt * 128:(kt + 1) * 128],
                     qT[64:128, hp, qc * 512:(qc + 1) * 512],
                     start=True, stop=True,
                     tile_position=(64, 0)).then_inc(sm["tM"], 1)

    def emit_av(j):
        hp, qc, kt = IT[j]
        if kt == 0 and n_evac >= 1:
            wait("tensor", "vM", n_evac)
        wait("tensor", "sM", exp_t[j])
        _, t1 = inc("tM")
        av_t[j] = t1

        @on("tensor")
        def _(e, hp=hp, kt=kt, b=j % 4):
            e.matmul(psO[0], v_aug[:, kt, 65 * (2 * hp):65 * (2 * hp) + 65],
                     PvA[b], start=(kt == 0), stop=(kt == 15),
                     skip_group_check=True)
            e.matmul(psO[1], v_aug[:, kt, 65 * (2 * hp + 1):65 * (2 * hp + 1) + 65],
                     PvB[b], start=(kt == 0), stop=(kt == 15),
                     skip_group_check=True).then_inc(sm["tM"], 1)

    def emit_evac(hp, qc):
        wait("vector", "tM", C["tM"])
        if qc == 0:
            wait("vector", "p5v", mult_done[hp - 1])
        slot = (n_evac * 2) % 4
        _, v_t = inc("vM")

        @on("vector")
        def _(e, hp=hp, qc=qc, slot=slot):
            e.tensor_copy(s_cb[0:64, qc * 512:(qc + 1) * 512], psO[0][0:64, :])
            e.tensor_copy(s_cb[64:128, qc * 512:(qc + 1) * 512], psO[1][0:64, :])
            e.tensor_copy(s_sumsc[0:1, slot * 512:(slot + 1) * 512],
                          psO[0][64:65, :])
            e.tensor_copy(s_sumsc[0:1, (slot + 1) * 512:(slot + 2) * 512],
                          psO[1][64:65, :]).then_inc(sm["vM"], 1)
        wait("sync", "vM", n_evac + 1)
        for ab in range(2):
            hh = 2 * hp + ab
            ci = hh * 4 + qc
            sydma(sums_d[2 * ci:2 * ci + 2, :].rearrange("a b -> (a b)")
                  .rearrange("(o n) -> o n", o=1),
                  s_sumsc[0:1, (slot + ab) * 512:(slot + ab + 1) * 512])
        if n_evac >= 1:
            wait("vector", dSY, C["dSY"] - 32)
        if qc == 3:
            pair_sums_t[hp] = C["dSY"]

    PROJ_AT = {2: ("q", 0), 8: ("q", 1), 14: ("q", 2), 20: ("q", 3),
               26: ("k", 0), 32: ("k", 1), 38: ("k", 2), 44: ("k", 3)}
    for i in range(NIT + 1):
        hp_i = IT[min(i, NIT - 1)][0]
        k_in_pair = i - 64 * hp_i if i < NIT else 64
        if i < NIT:
            if i % 64 == 0:
                wait("scalar", "stV", stab_done[hp_i])
            emit_scores(i)
        if i >= 1:
            j = i - 1
            emit_av(j)
            if IT[j][2] == 15:
                emit_evac(IT[j][0], IT[j][1])
                n_evac += 1
                if IT[j][1] == 3:
                    emit_pair_recip_chain(IT[j][0])
        if i < NIT:
            wait("scalar", "tM", sc_t[i])
            if i >= 4:
                wait("scalar", "tM", av_t[i - 4])
            inc("sM", 1)
            exp_t[i] = C["sM"]

            @on("scalar")
            def _(e, b=i % 2, pb=i % 4, hp=hp_i):
                e.activation(PvAB[pb], psS2[b], AF.Exp,
                             bias=s_bB[:, hp:hp + 1]).then_inc(sm["sM"], 1)
        if i < NIT and hp_i < HP - 1:
            nxt = hp_i + 1
            if k_in_pair in PROJ_AT:
                which, ch = PROJ_AT[k_in_pair]
                emit_proj_group(nxt, which, ch)
                if ch == 3:
                    qk_part[(nxt, which)] = C["p2v"]
            elif k_in_pair == 22:
                emit_stab_squares(nxt, "q")
            elif k_in_pair == 24:
                emit_stab_norms(nxt, "q")
            elif k_in_pair == 46:
                emit_stab_squares(nxt, "k")
            elif k_in_pair == 48:
                emit_stab_norms(nxt, "k")
            elif k_in_pair == 52:
                emit_stab_tail(nxt)
        if i >= 8 and (i - 8) % 64 == 0 and (i - 8) // 64 - 1 in rb_emit:
            emit_rb_mults((i - 8) // 64 - 1)
    emit_rb_mults(HP - 1)
    P4_TM = C["tM"]
    P5_V_DONE = C["p5v"]
    P5_T_DONE = C["p5t"]

    load_weight("woT", 0, dep=("p2t", C["p2t"]))
    aWoT = sWt[0]
    wait("tensor", "vW", wcast_done["woT"])

    t_bo = sydma(s_row, bias_d["bo"][:].rearrange("(o n) -> o n", o=1))
    wait("tensor", "p5v", P5_V_DONE)
    BO_BC_V = bias_bcast(t_bo)

    wait("vector", "p2v", BO_BC_V)
    rq_t6 = {st: C["rq"] + st + 1 for st in range(ST)}
    rs_t6 = {st: C["rs"] + st + 1 for st in range(ST)}
    ln_t6 = {st: C["sCh"] + st + 1 for st in range(ST)}
    C["rq"] += ST
    C["rs"] += ST
    C["sCh"] += ST
    for st in range(ST + 1):
        @on("scalar")
        def _(e, st=st):
            if st >= 1:
                p = 16 * ((st - 1) % 2)
                e.wait_ge(sm["sCh"], ln_t6[st - 1])
                e.activation(s_stat[:, p + 14:p + 15], s_stat[:, p + 15:p + 16],
                             AF.Exp, scale=-0.5).then_inc(sm["rs"], 1)
            if st < ST:
                p = 16 * (st % 2)
                e.wait_ge(sm["rq"], rq_t6[st])
                e.activation(s_stat[:, p + 15:p + 16], s_stat[:, p + 13:p + 14],
                             AF.Ln, bias=s_eps[:, :]).then_inc(sm["sCh"], 1)
    x1_dma = []
    xr_dma = []
    P6A0 = C["p6a"]
    for st in range(ST + 1):
        if st >= 1:
            stp = st - 1
            p = 16 * (stp % 2)
            wait("vector", "rs", rs_t6[stp])
            wait("vector", dSY, xr_dma[stp])
            _, m1 = inc("vCh")
            _, e_t = inc("p6e")

            _, m2 = inc("vCh")

            @on("vector")
            def _(e, stp=stp, p=p, m1=m1, m2=m2):
                scr = s_scr[:, (stp % 2) * 1024:(stp % 2 + 1) * 1024]
                xt = s_x[:, (stp % 2) * 1024:(stp % 2 + 1) * 1024]
                e.tensor_scalar(s_stat[:, p + 15:p + 16], s_stat[:, p + 12:p + 13],
                                s_stat[:, p + 14:p + 15], -1.0,
                                op0=ALU.mult, op1=ALU.mult).then_inc(sm["vCh"], 1)
                e.wait_ge(sm["vCh"], m1)
                e.tensor_scalar(scr, scr, s_stat[:, p + 14:p + 15],
                                s_stat[:, p + 15:p + 16],
                                op0=ALU.mult, op1=ALU.add).then_inc(sm["vCh"], 1)
                e.wait_ge(sm["vCh"], m2)
                e.tensor_tensor(xt, scr, xt, op=ALU.add).then_inc(sm["p6e"], 1)
            wait("gpsimd", "p6e", e_t)
            x1_dma.append(gdma(x1_d[stp * 128:(stp + 1) * 128, :],
                               s_x[:, (stp % 2) * 1024:(stp % 2 + 1) * 1024]))
        if st < ST:
            if st >= 2:
                wait("sync", "p6e", st - 1)
                wait("sync", dG, x1_dma[st - 2])
            xr_dma.append(sydma(s_x[:, (st % 2) * 1024:(st % 2 + 1) * 1024],
                                x_d[st * 128:(st + 1) * 128, :]))
            pb = 2 if st % 2 == 0 else 4
            if st >= 2:
                wait("tensor", "p6a", P6A0 + st - 1)
            _, t_t = inc("p6t")

            @on("tensor")
            def _(e, st=st, pb=pb):
                for ch in range(2):
                    for et in range(ET):
                        ins = e.matmul(bank(pb + ch),
                                       oTn[:, et, st * 128:(st + 1) * 128],
                                       aWoT[:, et, ch * 512:(ch + 1) * 512],
                                       start=(et == 0), stop=(et == ET - 1))
                ins.then_inc(sm["p6t"], 1)
            wait("vector", "p6t", t_t)
            _, a_t = inc("p6a")
            _, sb_v = inc("vCh")

            @on("vector")
            def _(e, st=st, pb=pb, sb_v=sb_v, a_t=a_t):
                p = 16 * (st % 2)
                scr = s_scr[:, (st % 2) * 1024:(st % 2 + 1) * 1024]
                e.tensor_tensor(scr, bank2(pb), s_bb[:, :],
                                op=ALU.add).then_inc(sm["p6a"], 1)
                e.wait_ge(sm["p6a"], a_t)
                e.bn_stats(s_stat[:, p:p + 6], scr[:, 0:512])
                e.bn_stats(s_stat[:, p + 6:p + 12],
                           scr[:, 512:1024]).then_inc(sm["vCh"], 1)
                e.wait_ge(sm["vCh"], sb_v)
                e.bn_aggr(s_stat[:, p + 12:p + 14],
                          s_stat[:, p:p + 12]).then_inc(sm["rq"], 1)
    P6_T_DONE = C["p6t"]

    wait("gpsimd", dG, C["dG"])

    @on("gpsimd")
    def _(e):
        with allow_ici_all_to_all():
            e.collective_compute(
                "AllToAll", ALU.bypass, replica_groups=[list(range(8))],
                ins=[x1_d[:, :]], outs=[x1g_d[:, :]]).then_inc(ccS)

    load_weight("fwcT", wcTb, queue=scdma, dep=("tM", P4_TM))
    load_weight("fucT", ucTb, queue=scdma)
    load_weight("fwiT", wiTb, queue=scdma)
    load_weight("fuiT", uiTb, queue=scdma)
    load_weight("fwfT", wfTb, queue=scdma)
    load_weight("fufT", ufTb, queue=scdma)
    load_weight("fwoT", woTb, queue=scdma)
    load_weight("fuoT", uoTb, queue=scdma, dep=("p6t", P6_T_DONE))
    FFN_CAST_DONE = C["vW"]

    wait("sync", ccS, 1)
    xg_dma = []
    for st in range(ST):
        if st >= 2:
            wait("sync", "p8a", st - 1)
        xg_dma.append(sydma(s_x[:, (st % 2) * 1024:(st % 2 + 1) * 1024],
                            x1g_d[st * 128:(st + 1) * 128, :]))
    wait("vector", "p6t", P6_T_DONE)
    ln_transpose_pipe(xlnFT, xg_dma, dSY, "p8a", "p8tr", "p8e", 0, 0)

    wait("vector", dSY, FB_T)
    wait("vector", "p5v", P5_V_DONE)
    _, ci1 = inc("vCh")
    _, ci2 = inc("vCh")
    _, p9v_init = inc("p9v")

    @on("vector")
    def _(e, ci1=ci1, ci2=ci2):
        for et in range(ET):
            ins = e.memset(cT[:, et, :], 0.0)
        ins.then_inc(sm["vCh"], 1)
        e.wait_ge(sm["vCh"], ci1)
        for et in range(ET):
            ins = e.tensor_scalar(cT[:, et, :], cT[:, et, :],
                                  s_bp[:, BP["finc"] * 8 + et:
                                       BP["finc"] * 8 + et + 1],
                                  None, op0=ALU.add)
        ins.then_inc(sm["vCh"], 1)
        e.wait_ge(sm["vCh"], ci2)
        for et in range(ET):
            ins = e.tensor_copy(cTb[:, et, :], cT[:, et, :])
        ins.then_inc(sm["p9v"], 1)

    GFUNC = {"f": AF.Sigmoid, "i": AF.Sigmoid, "c": AF.Tanh}
    GW = {"f": (wfTb, ufTb, "fbf"), "i": (wiTb, uiTb, "fbi"),
          "c": (wcTb, ucTb, "fbc")}
    wait("tensor", "p8e", C["p8e"])
    wait("tensor", "vW", FFN_CAST_DONE)
    wait("tensor", "p9v", p9v_init)
    wait("scalar", dSY, FB_T)
    gidx = 0
    for t in range(NT):
        for ft in range(ET):
            for g in ["f", "i", "c"]:
                slot = gidx % 6
                if gidx >= 6:
                    wait("tensor", "p9s", gidx - 5)
                _, t_t = inc("p9t")

                @on("tensor")
                def _(e, t=t, ft=ft, g=g, slot=slot,
                      first=(ft == 0 and g == "f")):
                    wT, uT, _b = GW[g]
                    for et in range(ET):
                        e.matmul(bank(slot)[:, 0:256],
                                 wT[:, et, ft * 128:(ft + 1) * 128],
                                 xlnFT[:, et, t * 256:(t + 1) * 256],
                                 start=(et == 0), stop=False)
                    if first and t >= 1:
                        e.wait_ge(sm["p9v"], p9v_init + t)
                    for et in range(ET):
                        ins = e.matmul(bank(slot)[:, 0:256],
                                       uT[:, et, ft * 128:(ft + 1) * 128],
                                       cTb[:, et, :],
                                       start=False, stop=(et == ET - 1))
                    ins.then_inc(sm["p9t"], 1)
                wait("scalar", "p9t", t_t)
                bcol = BP[GW[g][2]] * 8 + ft
                _, s_t = inc("p9s")

                @on("scalar")
                def _(e, g=g, ft=ft, slot=slot, bcol=bcol):
                    e.activation(gate[g][:, ft, :], bank(slot)[:, 0:256],
                                 GFUNC[g], bias=s_bp[:, bcol:bcol + 1]
                                 ).then_inc(sm["p9s"], 1)
                gidx += 1
            wait("vector", "p9s", C["p9s"])
            _, c1 = inc("vCh")
            _, c2 = inc("vCh")

            @on("vector")
            def _(e, ft=ft, c1=c1, c2=c2):
                e.wait_ge(sm["vCh"], c1 - 1)
                e.tensor_tensor(g_tmp, gate["i"][:, ft, :], gate["c"][:, ft, :],
                                op=ALU.mult)
                e.tensor_tensor(s_tmpf, gate["f"][:, ft, :], cT[:, ft, :],
                                op=ALU.mult).then_inc(sm["vCh"], 1)
                e.wait_ge(sm["vCh"], c1)
                e.tensor_tensor(cT[:, ft, :], s_tmpf, g_tmp,
                                op=ALU.add).then_inc(sm["vCh"], 1)
        wait("vector", "p9t", C["p9t"])
        wait("vector", "vCh", C["vCh"])
        _, v_t = inc("p9v")

        @on("vector")
        def _(e):
            for ft in range(ET):
                ins = e.tensor_copy(cTb[:, ft, :], cT[:, ft, :])
            ins.then_inc(sm["p9v"], 1)
    P9_V_DONE = C["p9v"]

    wait("sync", "p8e", C["p8e"])
    t_bo2 = sydma(s_row, bias_d["fbo"][:].rearrange("(o n) -> o n", o=1))
    wait("tensor", "p9v", P9_V_DONE)
    BO2_BC_V = bias_bcast(t_bo2)

    wait("vector", "p2v", BO2_BC_V)
    U_A0 = C["p10a"]
    for pt in range(2):
        for ch in range(2):
            g = pt * 2 + ch
            pb = g % 2
            if g >= 2:
                wait("tensor", "p10a", U_A0 + g - 1)
            _, t_t = inc("p10t")

            @on("tensor")
            def _(e, pt=pt, ch=ch, pb=pb):
                for et in range(ET):
                    ins = e.matmul(bank(pb), cTb[:, et, pt * 128:(pt + 1) * 128],
                                   uoTb[:, et, ch * 512:(ch + 1) * 512],
                                   start=(et == 0), stop=(et == ET - 1))
                ins.then_inc(sm["p10t"], 1)
            wait("vector", "p10t", t_t)
            _, a_t = inc("p10a")

            @on("vector")
            def _(e, pt=pt, ch=ch, pb=pb):
                e.tensor_tensor(U_plus[:, pt, ch * 512:(ch + 1) * 512], bank(pb),
                                s_bb[:, ch * 512:(ch + 1) * 512],
                                op=ALU.add).then_inc(sm["p10a"], 1)
    U_A_DONE = C["p10a"]

    wait("sync", "p9v", P9_V_DONE)
    rq_ta = {i: C["rq"] + i + 1 for i in range(ST)}
    rs_ta = {i: C["rs"] + i + 1 for i in range(ST)}
    ln_ta = {i: C["sCh"] + i + 1 for i in range(ST)}
    C["rq"] += ST
    C["rs"] += ST
    C["sCh"] += ST
    tanh_t = {}
    add_t_hold = {}
    res_dma = []
    out_dma = []
    for i in range(ST + 2):
        if i < ST:
            _, h_t = inc("p10h")
            tanh_t[i] = h_t

        @on("scalar")
        def _(e, i=i):
            if i >= 2:
                p = 16 * ((i - 2) % 2)
                e.wait_ge(sm["sCh"], ln_ta[i - 2])
                e.activation(s_stat[:, p + 14:p + 15], s_stat[:, p + 15:p + 16],
                             AF.Exp, scale=-0.5).then_inc(sm["rs"], 1)
            if i >= 1 and i - 1 < ST:
                p = 16 * ((i - 1) % 2)
                e.wait_ge(sm["rq"], rq_ta[i - 1])
                e.activation(s_stat[:, p + 15:p + 16], s_stat[:, p + 13:p + 14],
                             AF.Ln, bias=s_eps[:, :]).then_inc(sm["sCh"], 1)
            if i < ST:
                e.wait_ge(sm["p10a"], add_t_hold[i])
                scr = s_scr[:, (i % 2) * 1024:(i % 2 + 1) * 1024]
                e.activation(scr, scr, AF.Tanh).then_inc(sm["p10h"], 1)
        if i >= 2:
            ip = i - 2
            p = 16 * (ip % 2)
            wait("vector", "rs", rs_ta[ip])
            wait("vector", dSY, res_dma[ip])
            _, m1 = inc("vCh")
            _, m2 = inc("vCh")
            _, e_t = inc("p10e")

            @on("vector")
            def _(e, ip=ip, p=p, m1=m1, m2=m2):
                scr = s_scr[:, (ip % 2) * 1024:(ip % 2 + 1) * 1024]
                xt = s_x[:, (ip % 2) * 1024:(ip % 2 + 1) * 1024]
                e.tensor_scalar(s_stat[:, p + 15:p + 16], s_stat[:, p + 12:p + 13],
                                s_stat[:, p + 14:p + 15], -1.0,
                                op0=ALU.mult, op1=ALU.mult).then_inc(sm["vCh"], 1)
                e.wait_ge(sm["vCh"], m1)
                e.tensor_scalar(scr, scr, s_stat[:, p + 14:p + 15],
                                s_stat[:, p + 15:p + 16],
                                op0=ALU.mult, op1=ALU.add).then_inc(sm["vCh"], 1)
                e.wait_ge(sm["vCh"], m2)
                e.tensor_tensor(xt, scr, xt, op=ALU.add).then_inc(sm["p10e"], 1)
            wait("gpsimd", "p10e", e_t)
            out_dma.append(gdma(out_d[ip // 2, (ip % 2) * 128:(ip % 2) * 128 + 128, :],
                                s_x[:, (ip % 2) * 1024:(ip % 2 + 1) * 1024]))
        if i >= 1 and i - 1 < ST:
            ib = i - 1
            p = 16 * (ib % 2)
            wait("vector", "p10h", tanh_t[ib])
            _, sb_v = inc("vCh")

            @on("vector")
            def _(e, ib=ib, p=p, sb_v=sb_v):
                scr = s_scr[:, (ib % 2) * 1024:(ib % 2 + 1) * 1024]
                e.bn_stats(s_stat[:, p:p + 6], scr[:, 0:512])
                e.bn_stats(s_stat[:, p + 6:p + 12],
                           scr[:, 512:1024]).then_inc(sm["vCh"], 1)
                e.wait_ge(sm["vCh"], sb_v)
                e.bn_aggr(s_stat[:, p + 12:p + 14],
                          s_stat[:, p:p + 12]).then_inc(sm["rq"], 1)
        if i < ST:
            if i >= 2:
                wait("sync", "p10e", i - 1)
                wait("sync", dG, out_dma[i - 2])
            res_dma.append(sydma(s_x[:, (i % 2) * 1024:(i % 2 + 1) * 1024],
                                 x1g_d[i * 128:(i + 1) * 128, :]))
            t, st2 = i // 2, i % 2
            pb = 2 if i % 2 == 0 else 4
            if i >= 2:
                wait("tensor", "p10a", U_A_DONE + i - 1)
            _, t_t = inc("p10t")

            @on("tensor")
            def _(e, t=t, st2=st2, pb=pb):
                for ch in range(2):
                    for et in range(ET):
                        ins = e.matmul(
                            bank(pb + ch),
                            xlnFT[:, et,
                                  t * 256 + st2 * 128:t * 256 + st2 * 128 + 128],
                            woTb[:, et, ch * 512:(ch + 1) * 512],
                            start=(et == 0), stop=(et == ET - 1))
                ins.then_inc(sm["p10t"], 1)
            wait("vector", "p10t", t_t)
            _, a_t = inc("p10a")
            add_t_hold[i] = a_t

            @on("vector")
            def _(e, i=i, st2=st2, pb=pb):
                scr = s_scr[:, (i % 2) * 1024:(i % 2 + 1) * 1024]
                e.tensor_tensor(scr, bank2(pb), U_plus[:, st2, :],
                                op=ALU.add).then_inc(sm["p10a"], 1)

    wait("gpsimd", dG, C["dG"])
    wait("sync", dSY, C["dSY"])
    wait("scalar", dS2, C["dS2"])

    with es, nc.Block() as blk:
        @blk.tensor
        def _(e):
            for f in ops["tensor"]:
                f(e)

        @blk.scalar
        def _(e):
            for f in ops["scalar"]:
                f(e)

        @blk.vector
        def _(e):
            for f in ops["vector"]:
                f(e)

        @blk.gpsimd
        def _(e):
            for f in ops["gpsimd"]:
                f(e)

        @blk.sync
        def _(e):
            for f in ops["sync"]:
                f(e)

    return nc


_NC_CACHE = None


def kernel(L0_attn_x, L0_attn_Wq, L0_attn_bq, L0_attn_Wk, L0_attn_bk,
           L0_attn_Wv, L0_attn_bv, L0_attn_Wo, L0_attn_bo,
           L0_ffn_wf, L0_ffn_uf, L0_ffn_bf, L0_ffn_wi, L0_ffn_ui, L0_ffn_bi,
           L0_ffn_wc, L0_ffn_uc, L0_ffn_bc, L0_ffn_wo, L0_ffn_uo, L0_ffn_bo,
           L0_ffn_in_c):
    global _NC_CACHE
    if _NC_CACHE is None:
        _NC_CACHE = build()
    nc = _NC_CACHE

    tr = lambda w: np.ascontiguousarray(np.asarray(w).T)
    f32c = lambda v: np.ascontiguousarray(np.asarray(v), dtype=np.float32)
    common = {
        "wqT": tr(L0_attn_Wq), "wkT": tr(L0_attn_Wk), "wvT": tr(L0_attn_Wv),
        "woT": tr(L0_attn_Wo),
        "fwfT": tr(L0_ffn_wf), "fufT": tr(L0_ffn_uf),
        "fwiT": tr(L0_ffn_wi), "fuiT": tr(L0_ffn_ui),
        "fwcT": tr(L0_ffn_wc), "fucT": tr(L0_ffn_uc),
        "fwoT": tr(L0_ffn_wo), "fuoT": tr(L0_ffn_uo),
        "b_bq": f32c(L0_attn_bq), "b_bk": f32c(L0_attn_bk),
        "b_bv": f32c(L0_attn_bv), "b_bo": f32c(L0_attn_bo),
        "b_fbf": f32c(L0_ffn_bf), "b_fbi": f32c(L0_ffn_bi),
        "b_fbc": f32c(L0_ffn_bc), "b_fbo": f32c(L0_ffn_bo),
        "b_finc": f32c(L0_ffn_in_c),
    }
    x = np.asarray(L0_attn_x)
    in_maps = [dict(common, x=np.ascontiguousarray(x[b])) for b in range(8)]
    res = run_bass_kernel_spmd(nc, in_maps, core_ids=list(range(8)))
    out = np.concatenate([res.results[p]["out"] for p in range(8)], axis=1)
    return out
